# version 05
# baseline (speedup 1.0000x reference)
import numpy as np
import ml_dtypes
import concourse.bass as bass
import concourse.mybir as mybir
from concourse.bass_utils import run_bass_kernel_spmd

F32 = mybir.dt.float32
BF16 = mybir.dt.bfloat16
AX = mybir.AxisListType.X
ALU = mybir.AluOpType
ACT = mybir.ActivationFunctionType

NCORE = 8
B, S, D, H, HD = 8, 2048, 1024, 16, 64
SO = S // NCORE
T = B * SO
DC = D // 128
TC = T // 128
ES, FF, GRP = 128, 32, 32
DH = 4096
RNE = float(np.float32(3 * 2 ** 22))
BF = ml_dtypes.bfloat16


class Sem:
    def __init__(self, h):
        self.h = h
        self.n = 0


def build(debug=False):
    nc = bass.Bass("TRN2", target_bir_lowering=False, debug=False,
                   detect_race_conditions=False)

    def inp(name, shape, dt=F32):
        return nc.dram_tensor(name, shape, dt, kind="ExternalInput").ap()

    xt = inp("xt", [D, T])
    bt = inp("bt", [H, S, SO])
    wq0 = inp("wq0", [D, D]); wk0 = inp("wk0", [D, D]); wv0 = inp("wv0", [D, D])
    g0 = inp("g0", [128, DC]); b0 = inp("b0", [128, DC])
    wc = inp("wc", [D, ES]); bcb = inp("bcb", [128, 1])
    w1s = inp("w1s", [4, DC, 128, 32, FF])
    b1p = inp("b1p", [128, ES])
    w2s = inp("w2s", [4, 128, 16, DC, 128])
    b2p = inp("b2p", [ES, D])
    wq1 = inp("wq1", [D, D]); wk1 = inp("wk1", [D, D])
    wv1 = inp("wv1", [D, D]); wp1 = inp("wp1", [D, D])
    bq1 = inp("bq1", [128, DC]); bp1 = inp("bp1", [128, DC])
    bk1r = inp("bk1r", [1, D]); bv1r = inp("bv1r", [1, D])
    w1t = inp("w1t", [D, DH])
    w2t = inp("w2t", [DH, D])
    cvw = inp("cvw", [128, 32, 3])
    cvb = inp("cvb", [128, 32])
    hsel = inp("hsel", [128, 16])
    out_own = nc.dram_tensor("out_own", [D, T], F32, kind="ExternalOutput").ap()

    dbg = "ExternalOutput" if debug else "Internal"
    kv_in = nc.dram_tensor("kv_in", [D * T + T * H * 65], BF16)
    kv_all = nc.dram_tensor("kv_all", [NCORE, D * T + T * H * 65], BF16)
    att_d = nc.dram_tensor("att_d", [D, B, SO], F32, kind=dbg)
    r1t_d = nc.dram_tensor("r1t_d", [D, T], F32, kind=dbg)
    ffn_d = nc.dram_tensor("ffn_d", [D, T], F32, kind=dbg)
    r2t_d = nc.dram_tensor("r2t_d", [D, T], F32, kind=dbg)
    r3t_d = nc.dram_tensor("r3t_d", [D, T], F32, kind=dbg)
    kvs_in = nc.dram_tensor("kvs_in", [B * H * HD * HD], F32)
    kvs_out = nc.dram_tensor("kvs_out", [B * H * HD * HD], F32)
    halo_in = nc.dram_tensor("halo_in", [16, DH], BF16)
    halo_out = nc.dram_tensor("halo_out", [NCORE * 16, DH], BF16)
    w1q_d = nc.dram_tensor("w1q_d", [DC, 128, DH], BF16)
    w2q_d = nc.dram_tensor("w2q_d", [32, 128, D], BF16)
    at1_d = nc.dram_tensor("at1_d", [D, T], BF16)
    row_d = nc.dram_tensor("row_d", [4096], F32)
    bt_b = nc.dram_tensor("bt_b", [H * S * SO], BF16)
    wq0_b = nc.dram_tensor("wq0_b", [D * D], BF16)
    wk0_b = nc.dram_tensor("wk0_b", [D * D], BF16)
    wv0_b = nc.dram_tensor("wv0_b", [D * D], BF16)
    wq1_b = nc.dram_tensor("wq1_b", [D * D], BF16)
    wk1_b = nc.dram_tensor("wk1_b", [D * D], BF16)
    wv1_b = nc.dram_tensor("wv1_b", [D * D], BF16)
    wp1_b = nc.dram_tensor("wp1_b", [D * D], BF16)
    w1s_b = nc.dram_tensor("w1s_b", [4 * DC * 128 * 32 * FF], BF16)
    w2s_b = nc.dram_tensor("w2s_b", [4 * 128 * 16 * DC * 128], BF16)
    xt_b = nc.dram_tensor("xt_b", [D * T], BF16)
    dbg1 = nc.dram_tensor("dbg1", [128, 2048], F32, kind=dbg)
    dbg2 = nc.dram_tensor("dbg2", [128, 2048], F32, kind=dbg)
    dbg3 = nc.dram_tensor("dbg3", [128, 8192], BF16, kind=dbg)
    dbg4 = nc.dram_tensor("dbg4", [128, 512], BF16, kind=dbg)
    dbg5 = nc.dram_tensor("dbg5", [128, 32768], BF16, kind=dbg)
    kvi, kva = kv_in.ap(), kv_all.ap()

    persist = []
    def sbp(name, shape, dt):
        cm = nc.sbuf_tensor(name, shape, dt)
        t = cm.__enter__()
        persist.append(cm)
        return t

    identb = sbp("identb", [128, 128], BF16)
    identf = sbp("identf", [128, 128], F32)
    ones_f1 = sbp("ones_f1", [1, 128], F32)
    ones_ff = sbp("ones_ff", [128, 1], F32)
    tieb = sbp("tieb", [128, 2048], F32)
    rowa = sbp("rowa", [1, 2048], F32)
    rowb = sbp("rowb", [1, 2048], F32)
    misc = sbp("misc", [128, 512], F32)
    misc2 = sbp("misc2", [128, 512], F32)
    g0s = sbp("g0s", [128, DC], F32)
    b0s = sbp("b0s", [128, DC], F32)
    bcs = sbp("bcs", [128, 1], F32)
    b1s = sbp("b1s", [128, ES], F32)
    bq1s = sbp("bq1s", [128, DC], F32)
    bp1s = sbp("bp1s", [128, DC], F32)
    cvws = sbp("cvws", [128, 32, 3], F32)
    cvbs = sbp("cvbs", [128, 32], F32)
    hsels = sbp("hsels", [128, 16], BF16)

    from contextlib import ExitStack
    _top = ExitStack()
    _ec = _top.enter_context
    ps_a = _ec(nc.psum_tensor("ps_a", [128, 1024], F32))
    ps_b = _ec(nc.psum_tensor("ps_b", [128, 1024], F32))
    ps_s = _ec(nc.psum_tensor("ps_s", [128, 512], F32))
    ps_t = _ec(nc.psum_tensor("ps_t", [128, 2048], BF16))
    _sd = _ec(nc.semaphore("sd")); _sg = _ec(nc.semaphore("sg"))
    _st = _ec(nc.semaphore("st")); _sv = _ec(nc.semaphore("sv"))
    _ss = _ec(nc.semaphore("ss")); _sc = _ec(nc.semaphore("sc"))
    _so = _ec(nc.semaphore("so"))
    _sgd = _ec(nc.semaphore("sgd"))
    block = _ec(nc.Block())
    if True:
        sd, sg, st, sv, ss, sc, so = (Sem(_sd), Sem(_sg), Sem(_st), Sem(_sv),
                                      Sem(_ss), Sem(_sc), Sem(_so))
        sgd = Sem(_sgd)
        sems = [sd, sg, st, sv, ss, sc, so, sgd]
        S_, G_, T_, V_, C_ = nc.sync, nc.gpsimd, nc.tensor, nc.vector, nc.scalar
        ALLE = [S_, G_, T_, V_, C_]

        @block.sync
        def _(sync):
            def _wflush(e, xs):
                xs = list(xs)
                if sg in xs and sgd not in xs:
                    xs.append(sgd)
                pend = 0
                for i, x in enumerate(xs):
                    e.wait_ge(x.h, x.n)
                    pend += 1
                    if pend == 2 and i != len(xs) - 1:
                        e.sem_inc(xs[0].h, 0)
                        pend = 0

            def bar():
                for e in ALLE:
                    _wflush(e, sems)
                    e.sem_inc(sems[0].h, 0)

            def dS(out, in_, slow=False):
                S_.dma_start(out=out, in_=in_,
                             allow_slow_non_contiguous=slow
                             ).then_inc(sd.h, 16); sd.n += 16

            def dG(out, in_):
                G_.wait_ge(sg.h, sg.n)
                G_.wait_ge(sgd.h, sgd.n)
                G_.dma_start(out=out, in_=in_).then_inc(sgd.h, 16)
                sgd.n += 16

            def dO(out, in_):
                S_.dma_start(out=out, in_=in_).then_inc(so.h, 16); so.n += 16

            def tI(i):
                i.then_inc(st.h, 1); st.n += 1

            _lastT = {}

            def MM(*a, **k):
                i = T_.matmul(*a, **k)
                _lastT["i"] = i
                return i

            def TP(*a, **k):
                i = T_.transpose(*a, **k)
                _lastT["i"] = i
                return i

            def tmark():
                _lastT["i"].then_inc(st.h, 1); st.n += 1

            def vI(i):
                i.then_inc(sv.h, 1); sv.n += 1

            def cI(i):
                i.then_inc(ss.h, 1); ss.n += 1

            def wT(*xs):
                _wflush(T_, xs)

            def wV(*xs):
                _wflush(V_, xs)

            def wC(*xs):
                _wflush(C_, xs)

            def ln_fm(X, eps, sqt, sA, sB, gamma=None, beta=None):
                for half in range(2):
                    wT(sv, ss)
                    for dcx in range(DC):
                        for ntx in range(2):
                            c0 = half * 1024 + ntx * 512
                            MM(ps_a[0:1, ntx * 512:ntx * 512 + 512],
                                      ones_ff[:, :],
                                      X[:, dcx * 2048 + c0:dcx * 2048 + c0 + 512],
                                      start=(dcx == 0), stop=(dcx == DC - 1))
                    tmark()
                    wV(st)
                    vI(V_.tensor_copy(rowa[:, half * 1024:half * 1024 + 1024],
                                      ps_a[0:1, 0:1024]))
                for half in range(2):
                    wT(sv)
                    for dcx in range(DC):
                        c0 = half * 1024
                        if dcx == 0:
                            wC(sv)
                        cI(C_.activation(
                            sqt[:, 0:1024],
                            X[:, dcx * 2048 + c0:dcx * 2048 + c0 + 1024],
                            ACT.Square))
                        wT(ss)
                        for ntx in range(2):
                            MM(ps_b[0:1, ntx * 512:ntx * 512 + 512],
                                      ones_ff[:, :],
                                      sqt[:, ntx * 512:ntx * 512 + 512],
                                      start=(dcx == 0), stop=(dcx == DC - 1))
                        tmark()
                        wC(st)
                    wV(st)
                    vI(V_.tensor_copy(rowb[:, half * 1024:half * 1024 + 1024],
                                      ps_b[0:1, 0:1024]))
                vI(V_.tensor_scalar(out=rowa[:, :], in0=rowa[:, :],
                                    scalar1=1.0 / D, scalar2=None, op0=ALU.mult))
                vI(V_.tensor_scalar(out=rowb[:, :], in0=rowb[:, :],
                                    scalar1=1.0 / D, scalar2=None, op0=ALU.mult))
                vI(V_.tensor_tensor(sA[0:1, 0:2048], rowa[:, :], rowa[:, :],
                                    ALU.mult))
                vI(V_.tensor_tensor(rowb[:, :], rowb[:, :], sA[0:1, 0:2048],
                                    ALU.subtract))
                vI(V_.tensor_scalar(out=rowb[:, :], in0=rowb[:, :],
                                    scalar1=float(eps), scalar2=None, op0=ALU.add))
                wC(sv)
                cI(C_.activation(rowb[:, :], rowb[:, :], ACT.Sqrt))
                wV(ss)
                vI(V_.reciprocal(rowb[:, :], rowb[:, :]))
                vI(V_.tensor_tensor(rowa[:, :], rowa[:, :], rowb[:, :], ALU.mult))
                vI(V_.tensor_scalar(out=rowa[:, :], in0=rowa[:, :], scalar1=-1.0,
                                    scalar2=None, op0=ALU.mult))
                wT(sv)
                for half in range(2):
                    wT(ss)
                    for ntx in range(2):
                        c0 = half * 1024 + ntx * 512
                        MM(ps_a[:, ntx * 512:ntx * 512 + 512],
                                  ones_f1[:, :], rowb[0:1, c0:c0 + 512],
                                  start=True, stop=True)
                        MM(ps_b[:, ntx * 512:ntx * 512 + 512],
                                  ones_f1[:, :], rowa[0:1, c0:c0 + 512],
                                  start=True, stop=True)
                    tmark()
                    wC(st)
                    cI(C_.copy(sA[:, half * 1024:half * 1024 + 1024],
                               ps_a[:, 0:1024]))
                    cI(C_.copy(sB[:, half * 1024:half * 1024 + 1024],
                               ps_b[:, 0:1024]))
                wV(ss)
                for dcx in range(DC):
                    sl = X[:, dcx * 2048:(dcx + 1) * 2048]
                    vI(V_.tensor_tensor(sl, sl, sA[:, 0:2048], ALU.mult))
                    wV(sv)
                    vI(V_.tensor_tensor(sl, sl, sB[:, 0:2048], ALU.add))
                    if gamma is not None:
                        wC(sv)
                        cI(C_.activation(sl, sl, ACT.Identity,
                                         bias=beta[:, dcx:dcx + 1],
                                         scale=gamma[:, dcx:dcx + 1]))
                bar()

            G_.memset(identb[:, :], 0.0).then_inc(sg.h, 1); sg.n += 1
            G_.wait_ge(sg.h, sg.n)
            G_.affine_select(out=identb[:, :], in_=identb[:, :],
                             compare_op=ALU.not_equal, fill=1.0, base=0,
                             pattern=[[-1, 128]],
                             channel_multiplier=1).then_inc(sg.h, 1)
            sg.n += 1
            G_.wait_ge(sg.h, sg.n)
            G_.memset(identf[:, :], 0.0).then_inc(sg.h, 1); sg.n += 1
            G_.wait_ge(sg.h, sg.n)
            G_.affine_select(out=identf[:, :], in_=identf[:, :],
                             compare_op=ALU.not_equal, fill=1.0, base=0,
                             pattern=[[-1, 128]],
                             channel_multiplier=1).then_inc(sg.h, 1)
            sg.n += 1
            G_.wait_ge(sg.h, sg.n)
            G_.memset(ones_f1[:, :], 1.0).then_inc(sg.h, 1); sg.n += 1
            G_.wait_ge(sg.h, sg.n)
            G_.memset(ones_ff[:, :], 1.0).then_inc(sg.h, 1); sg.n += 1
            G_.wait_ge(sg.h, sg.n)
            G_.wait_ge(sgd.h, sgd.n)
            G_.iota(rowa[:, :], pattern=[[0, 64], [1, 32]], base=0,
                    channel_multiplier=0,
                    allow_small_or_imprecise_dtypes=True).then_inc(sg.h, 1)
            sg.n += 1
            wV(sg)
            vI(V_.tensor_scalar(out=rowa[:, :], in0=rowa[:, :],
                                scalar1=float(np.float32(1e-6) / np.float32(31.0)),
                                scalar2=None, op0=ALU.mult))
            dS(g0s[:, :], g0)
            dS(b0s[:, :], b0)
            dS(bcs[:, :], bcb)
            dS(b1s[:, :], b1p)
            dS(bq1s[:, :], bq1)
            dS(bp1s[:, :], bp1)
            dS(cvws[:, :, :], cvw)
            dS(cvbs[:, :], cvb)
            dG(hsels[:, :], hsel)
            wT(sv)
            for ntx in range(4):
                wT(ss)
                MM(ps_a[:, 0:512], ones_f1[:, :],
                          rowa[0:1, ntx * 512:(ntx + 1) * 512],
                          start=True, stop=True)
                tmark()
                wC(st)
                cI(C_.copy(tieb[:, ntx * 512:(ntx + 1) * 512], ps_a[:, 0:512]))
            bar()

            with (
                nc.sbuf_tensor("pf32", [128, 8192], F32) as pf32,
                nc.sbuf_tensor("pbf", [128, 8192], BF16) as pbf,
            ):
                jobs = []
                for srct, dstt in ((bt, bt_b), (wq0, wq0_b), (wk0, wk0_b),
                                   (wv0, wv0_b), (wq1, wq1_b), (wk1, wk1_b),
                                   (wv1, wv1_b), (wp1, wp1_b), (w1s, w1s_b),
                                   (w2s, w2s_b), (xt, xt_b)):
                    sf = 1
                    for dim in srct.shape:
                        sf *= dim
                    jobs.append((srct, dstt, sf))
                for srct, dstt, n_el in jobs:
                    flat_in = srct.rearrange(
                        " ".join("abcdefg"[:len(srct.shape)]) + " -> ("
                        + " ".join("abcdefg"[:len(srct.shape)]) + ")")
                    nt_ = n_el // (128 * 4096)
                    assert nt_ * 128 * 4096 == n_el, n_el
                    for k in range(nt_):
                        slot = (k % 2) * 4096
                        S_.wait_ge(sv.h, sv.n)
                        dS(pf32[:, slot:slot + 4096],
                           flat_in[k * 524288:(k + 1) * 524288]
                           .rearrange("(p q) -> p q", p=128))
                        wV(sd)
                        V_.wait_ge(sd.h, sd.n)
                        vI(V_.tensor_copy(pbf[:, slot:slot + 4096],
                                          pf32[:, slot:slot + 4096]))
                        S_.wait_ge(sv.h, sv.n)
                        dS(dstt.ap()[k * 524288:(k + 1) * 524288]
                           .rearrange("(p q) -> p q", p=128),
                           pbf[:, slot:slot + 4096])
                bar()

            _stk12 = ExitStack()
            xtb = _stk12.enter_context(nc.sbuf_tensor("xtb", [128, 16384], BF16))
            wt0 = _stk12.enter_context(nc.sbuf_tensor("wt0", [128, 8192], BF16))
            qt = _stk12.enter_context(nc.sbuf_tensor("qt", [128, 16384], BF16))
            ktv = _stk12.enter_context(nc.sbuf_tensor("ktv", [128, 16640], BF16))
            kto = _stk12.enter_context(nc.sbuf_tensor("kto", [128, 16384], BF16))
            if True:
                for dcx in range(DC):
                    dS(xtb[:, dcx * 2048:(dcx + 1) * 2048],
                       xt_b.ap()[dcx * 262144:(dcx + 1) * 262144]
                       .rearrange("(p q) -> p q", p=128))
                for dcx in range(DC):
                    dS(wt0[:, dcx * 1024:(dcx + 1) * 1024],
                       wq0_b.ap()[dcx * 131072:(dcx + 1) * 131072]
                       .rearrange("(p q) -> p q", p=128))
                bar()

                def proj_fm(dst, wsb, src):
                    ssb = ss.n
                    k = 0
                    for mcx in range(DC):
                        for ntx in range(2):
                            PS = ps_a if k % 2 == 0 else ps_b
                            if k >= 2:
                                T_.wait_ge(ss.h, ssb + k - 1)
                            for dcx in range(DC):
                                for nh in range(2):
                                    MM(
                                        PS[:, nh * 512:nh * 512 + 512],
                                        wsb[:, dcx * 1024 + mcx * 128:
                                            dcx * 1024 + (mcx + 1) * 128],
                                        src[:, dcx * 2048 + ntx * 1024 + nh * 512:
                                            dcx * 2048 + ntx * 1024 + nh * 512 + 512],
                                        start=(dcx == 0), stop=(dcx == DC - 1))
                            tmark()
                            wC(st)
                            cI(C_.copy(dst[:, mcx * 2048 + ntx * 1024:
                                           mcx * 2048 + ntx * 1024 + 1024],
                                       PS[:, 0:1024]))
                            k += 1

                proj_fm(qt, wt0, xtb)
                bar()
                for dcx in range(DC):
                    dS(wt0[:, dcx * 1024:(dcx + 1) * 1024],
                       wk0_b.ap()[dcx * 131072:(dcx + 1) * 131072]
                       .rearrange("(p q) -> p q", p=128))
                bar()
                proj_fm(kto, wt0, xtb)
                bar()
                for dcx in range(DC):
                    dS(wt0[:, dcx * 1024:(dcx + 1) * 1024],
                       wv0_b.ap()[dcx * 131072:(dcx + 1) * 131072]
                       .rearrange("(p q) -> p q", p=128))
                G_.wait_ge(sg.h, sg.n)
                G_.memset(ktv[:, :], 1.0).then_inc(sg.h, 1); sg.n += 1
                bar()
                ssb = ss.n
                for tcx in range(TC):
                    PS = ps_a if tcx % 2 == 0 else ps_b
                    if tcx >= 2:
                        T_.wait_ge(ss.h, ssb + tcx - 1)
                    for ntx in range(2):
                        for dcx in range(DC):
                            MM(PS[:, ntx * 512:(ntx + 1) * 512],
                                      xtb[:, dcx * 2048 + tcx * 128:
                                          dcx * 2048 + (tcx + 1) * 128],
                                      wt0[:, dcx * 1024 + ntx * 512:
                                          dcx * 1024 + (ntx + 1) * 512],
                                      start=(dcx == 0), stop=(dcx == DC - 1))
                    tmark()
                    wC(st)
                    cI(C_.copy(ktv[:, tcx * 1040:(tcx + 1) * 1040]
                               .rearrange("p (hh q) -> p hh q", q=65)[:, :, 0:64],
                               PS[:, 0:1024]
                               .rearrange("p (hh q) -> p hh q", q=64)))
                bar()
                for dcx in range(DC):
                    dS(kvi[dcx * 128 * T:(dcx + 1) * 128 * T]
                       .rearrange("(p t) -> p t", p=128),
                       kto[:, dcx * 2048:(dcx + 1) * 2048])
                for tcx in range(TC):
                    dS(kvi[D * T + tcx * 128 * 1040:
                           D * T + (tcx + 1) * 128 * 1040]
                       .rearrange("(p t) -> p t", p=128),
                       ktv[:, tcx * 1040:(tcx + 1) * 1040])
                bar()
                G_.wait_ge(sg.h, sg.n)
                G_.wait_ge(sgd.h, sgd.n)
                G_.collective_compute(
                    "AllGather", ALU.bypass, replica_groups=[list(range(NCORE))],
                    ins=[kvi], outs=[kva]).then_inc(sc.h, 1)
                sc.n += 1
                bar()

                KVT = D * T
                for b in range(B):
                    for src in range(NCORE):
                        for dcx in range(DC):
                            dS(xtb[:, dcx * 2048 + src * 256:
                                   dcx * 2048 + (src + 1) * 256],
                               kv_all[src, dcx * 128 * T:(dcx + 1) * 128 * T]
                               .rearrange("(p t) -> p t", p=128)
                               [:, b * 256:(b + 1) * 256])
                        for jcx in range(2):
                            kk = src * 2 + jcx
                            dS(ktv[:, kk * 1040:(kk + 1) * 1040],
                               kv_all[src, KVT + (b * 2 + jcx) * 128 * 1040:
                                      KVT + (b * 2 + jcx + 1) * 128 * 1040]
                               .rearrange("(p t) -> p t", p=128))
                    dS(wt0[:, 0:4096].rearrange("p (kc q) -> p kc q", kc=16),
                       bt_b.ap()[0:524288]
                       .rearrange("(kc p q) -> p kc q", p=128, q=256))
                    pf = {0: sd.n}
                    bar()
                    for h in range(H):
                        if h + 1 < H:
                            hb1 = ((h + 1) % 2) * 4096
                            S_.wait_ge(st.h, st.n - 1)
                            dS(wt0[:, hb1:hb1 + 4096]
                               .rearrange("p (kc q) -> p kc q", kc=16),
                               bt_b.ap()[(h + 1) * 524288:(h + 2) * 524288]
                               .rearrange("(kc p q) -> p kc q", p=128, q=256))
                            pf[h + 1] = sd.n
                        dcq, rq = h // 2, (h % 2) * 64
                        wT(sg, sv)
                        ss_base = ss.n
                        for kcx in range(16):
                            slt = (kcx % 2) * 512
                            if kcx >= 2:
                                T_.wait_ge(ss.h, ss_base + kcx - 1)
                            MM(ps_a[:, slt:slt + 256],
                                      xtb[rq:rq + 64, dcq * 2048 + kcx * 128:
                                          dcq * 2048 + (kcx + 1) * 128],
                                      qt[rq:rq + 64, dcq * 2048 + b * 256:
                                         dcq * 2048 + (b + 1) * 256],
                                      start=True, stop=True)
                            tmark()
                            wC(st)
                            cI(C_.activation(kto[:, kcx * 256:(kcx + 1) * 256],
                                             ps_a[:, slt:slt + 256], ACT.Exp,
                                             scale=1.0 / 32.0))
                        wT(ss, sv, sg)
                        for kcx in range(16):
                            MM(ps_b[0:65, 0:256],
                                      ktv[:, kcx * 1040 + h * 65:
                                          kcx * 1040 + h * 65 + 65],
                                      kto[:, kcx * 256:(kcx + 1) * 256],
                                      start=(kcx == 0), stop=(kcx == 15))
                        T_.wait_ge(sd.h, pf[h])
                        for kcx in range(16):
                            MM(ps_s[0:64, 0:256],
                                      ktv[:, kcx * 1040 + h * 65:
                                          kcx * 1040 + h * 65 + 64],
                                      wt0[:, (h % 2) * 4096 + kcx * 256:
                                          (h % 2) * 4096 + (kcx + 1) * 256],
                                      start=(kcx == 0), stop=(kcx == 15))
                        tmark()
                        wV(st)
                        V_.wait_ge(so.h, so.n)
                        vI(V_.reciprocal(rowa[0:1, 0:256], ps_b[64:65, 0:256]))
                        wT(sv)
                        MM(ps_a[0:64, 0:256], ones_f1[:, 0:64],
                                  rowa[0:1, 0:256], start=True, stop=True)
                        tmark()
                        wV(st)
                        vI(V_.tensor_copy(misc[0:64, 0:256], ps_a[0:64, 0:256]))
                        wV(sv)
                        vI(V_.tensor_tensor(misc2[0:64, 0:256], ps_b[0:64, 0:256],
                                            misc[0:64, 0:256], ALU.mult))
                        wV(sv)
                        vI(V_.tensor_tensor(misc2[0:64, 0:256],
                                            misc2[0:64, 0:256],
                                            ps_s[0:64, 0:256], ALU.add))
                        S_.wait_ge(sv.h, sv.n)
                        dO(att_d.ap()[h * 64:(h + 1) * 64, b, :],
                           misc2[0:64, 0:256])
                    bar()
            bar()

            _stk12.close()
            _stk4 = ExitStack()
            strm = _stk4.enter_context(nc.sbuf_tensor("strm", [128, 16384], F32))
            sqt = _stk4.enter_context(nc.sbuf_tensor("sqt", [128, 1024], F32))
            sA = _stk4.enter_context(nc.sbuf_tensor("sA", [128, 2048], F32))
            sB = _stk4.enter_context(nc.sbuf_tensor("sB", [128, 2048], F32))
            ev32 = _stk4.enter_context(nc.sbuf_tensor("ev32", [128, 2048], F32))
            if True:
                for dcx in range(DC):
                    for b in range(B):
                        dS(strm[:, dcx * 2048 + b * 256:
                                dcx * 2048 + (b + 1) * 256],
                           att_d.ap()[dcx * 128:(dcx + 1) * 128, b, :])
                bar()
                ln_fm(strm, 1e-3, sqt, sA, sB, gamma=g0s, beta=b0s)
                for dcx in range(DC):
                    S_.wait_ge(sv.h, sv.n)
                    dS(ev32[:, 0:2048], xt[dcx * 128:(dcx + 1) * 128, :])
                    wV(sd)
                    vI(V_.tensor_tensor(strm[:, dcx * 2048:(dcx + 1) * 2048],
                                        strm[:, dcx * 2048:(dcx + 1) * 2048],
                                        ev32[:, 0:2048], ALU.add))
                bar()
                ln_fm(strm, 1e-5, sqt, sA, sB)
                for dcx in range(DC):
                    dO(r1t_d.ap()[dcx * 128:(dcx + 1) * 128, :],
                       strm[:, dcx * 2048:(dcx + 1) * 2048])
                bar()
                _stk4.close()

                _stk5 = ExitStack()
                permb = _stk5.enter_context(nc.sbuf_tensor("permb", [128, 2048], BF16))
                permt = _stk5.enter_context(nc.sbuf_tensor("permt", [128, 2048], BF16))
                _stk5r = ExitStack()
                r1bk = _stk5r.enter_context(nc.sbuf_tensor("r1bk", [128, 16384], BF16))
                _stk5r2 = ExitStack()
                tmpc = _stk5r2.enter_context(nc.sbuf_tensor("tmpc", [128, 2048], BF16))
                if True:
                    for dcx in range(DC):
                        dG(tmpc[:, :], r1t_d.ap()[dcx * 128:(dcx + 1) * 128, :])
                        wT(sg, ss)
                        for tcx in range(TC):
                            slt = (tcx % 2) * 1024
                            wT(ss)
                            TP(ps_t[:, slt:slt + 128],
                               tmpc[:, tcx * 128:(tcx + 1) * 128],
                               identb[:, :])
                            tmark()
                            wC(st)
                            cI(C_.copy(r1bk[:, tcx * 1024 + dcx * 128:
                                            tcx * 1024 + (dcx + 1) * 128],
                                       ps_t[:, slt:slt + 128]))
                        bar()
                    _stk5r2.close()
                    bar()
                    _stk5l = ExitStack()
                    logit = _stk5l.enter_context(nc.sbuf_tensor("logit", [128, 2048], F32))
                    rt1 = _stk5l.enter_context(nc.sbuf_tensor("rt1", [128, 2048], F32))
                    rt2 = _stk5l.enter_context(nc.sbuf_tensor("rt2", [128, 2048], F32))
                    r1bs = _stk5l.enter_context(nc.sbuf_tensor("r1bs", [128, 512], BF16))
                    r1rs = _stk5l.enter_context(nc.sbuf_tensor("r1rs", [128, 512], BF16))
                    r1fs = _stk5l.enter_context(nc.sbuf_tensor("r1fs", [128, 512], F32))
                    wcb = _stk5l.enter_context(nc.sbuf_tensor("wcb", [128, 1024], BF16))
                    wcr = _stk5l.enter_context(nc.sbuf_tensor("wcr", [128, 1024], BF16))
                    fw = _stk5l.enter_context(nc.sbuf_tensor("fw", [128, 1024], F32))
                    if True:
                        for dcx in range(DC):
                            dS(fw[:, dcx * 128:(dcx + 1) * 128],
                               wc[dcx * 128:(dcx + 1) * 128, :])
                        bar()
                        vI(V_.tensor_copy(wcb[:, :], fw[:, :]))
                        wV(sv)
                        vI(V_.tensor_tensor(rt1[:, 0:1024], fw[:, :], wcb[:, :],
                                            ALU.subtract))
                        wV(sv)
                        vI(V_.tensor_copy(wcr[:, :], rt1[:, 0:1024]))
                        bar()
                        for ntx in range(4):
                            for dcx in range(DC):
                                wV(st)
                                V_.sem_inc(sd.h, 0)
                                V_.wait_ge(sd.h, sd.n)
                                G_.wait_ge(st.h, st.n)
                                S_.wait_ge(st.h, st.n)
                                S_.wait_ge(sv.h, sv.n)
                                dG(r1bs[:, :],
                                   r1t_d.ap()[dcx * 128:(dcx + 1) * 128,
                                              ntx * 512:(ntx + 1) * 512])
                                dS(r1fs[:, :],
                                   r1t_d.ap()[dcx * 128:(dcx + 1) * 128,
                                              ntx * 512:(ntx + 1) * 512])
                                V_.wait_ge(sgd.h, sgd.n)
                                V_.wait_ge(sd.h, sd.n)
                                V_.sem_inc(sd.h, 0)
                                vI(V_.tensor_tensor(r1rs[:, :], r1fs[:, :],
                                                    r1bs[:, :], ALU.subtract))
                                w_ = wcb[:, dcx * 128:(dcx + 1) * 128]
                                w2_ = wcr[:, dcx * 128:(dcx + 1) * 128]
                                wT(sv, sd, sg)
                                T_.wait_ge(ss.h, ss.n)
                                MM(ps_a[:, 0:512], w_, r1bs[:, :],
                                          start=(dcx == 0), stop=False)
                                MM(ps_a[:, 0:512], w_, r1rs[:, :],
                                          start=False, stop=False)
                                MM(ps_a[:, 0:512], w2_, r1bs[:, :],
                                          start=False, stop=(dcx == DC - 1))
                                tmark()
                            wC(st)
                            cI(C_.activation(logit[:, ntx * 512:(ntx + 1) * 512],
                                             ps_a[:, 0:512], ACT.Identity,
                                             bias=bcs[:, 0:1], scale=1.0))
                        bar()
                    wV(ss)
                    vI(V_.tensor_tensor(logit[:, :], logit[:, :], tieb[:, :],
                                        ALU.add))
                    wV(sv)
                    vI(V_.tensor_reduce(rt1[:, 0:64],
                                        logit[:, :].rearrange(
                                            "p (n t) -> p n t", t=32),
                                        AX, ALU.max))
                    wV(sv)
                    vI(V_.tensor_tensor(
                        rt2[:, 0:2048].rearrange("p (n t) -> p n t", t=32),
                        logit[:, :].rearrange("p (n t) -> p n t", t=32),
                        rt1[:, 0:64][:, :, None].broadcast_to([128, 64, 32]),
                        ALU.is_ge))
                    wV(sv)
                    vI(V_.tensor_copy(permb[:, :], rt2[:, 0:2048]))
                    if debug:
                        S_.wait_ge(sv.h, sv.n)
                        dO(dbg1.ap(), logit[:, :])
                        dO(dbg2.ap(), rt2[:, 0:2048])
                    bar()
                    _stk5l.close()
                    for tcx in range(TC):
                        sl8 = (tcx % 2) * 1024
                        wT(ss)
                        TP(ps_t[:, sl8:sl8 + 128],
                                     permb[:, tcx * 128:(tcx + 1) * 128],
                                     identb[:, :])
                        tmark()
                        wC(st)
                        cI(C_.copy(permt[:, tcx * 128:(tcx + 1) * 128],
                                   ps_t[:, sl8:sl8 + 128]))
                    bar()
                    _stk5b = ExitStack()
                    gth = _stk5b.enter_context(nc.sbuf_tensor("gth", [128, 8192], BF16))
                    rpt = _stk5b.enter_context(nc.sbuf_tensor("rpt", [128, 32768], BF16))
                    w1b = _stk5b.enter_context(nc.sbuf_tensor("w1b", [128, 8192], BF16))
                    w2b = _stk5b.enter_context(nc.sbuf_tensor("w2b", [128, 8192], BF16))
                    innr = _stk5b.enter_context(nc.sbuf_tensor("innr", [128, 512], BF16))
                    bdrh = _stk5b.enter_context(nc.sbuf_tensor("bdrh", [128, 256], BF16))
                    b2b = _stk5b.enter_context(nc.sbuf_tensor("b2b", [128, 1024], BF16))
                    rpg = _stk5b.enter_context(nc.sbuf_tensor("rpg", [128, 1024], BF16))
                    if True:
                        dG(b2b[:, :], b2p)
                        vI(V_.memset(innr[:, :], 0.0))
                        bar()
                        for half in range(2):
                            for j in range(4):
                                S_.wait_ge(st.h, st.n)
                                dS(w1b[:, :].rearrange(
                                       "p (a q) -> p a q", a=DC),
                                   w1s_b.ap()[j * 1048576:(j + 1) * 1048576]
                                   .rearrange("(a p q) -> p a q", a=DC, p=128))
                                G_.wait_ge(sg.h, sg.n)
                                G_.memset(bdrh[:, :], 0.0).then_inc(sg.h, 1)
                                sg.n += 1
                                bar()
                                for tcl in range(8):
                                    tcx = half * 8 + tcl
                                    eb = (tcl % 2) * 128
                                    wV(st)
                                    for gg in range(4):
                                        vI(V_.tensor_copy(
                                            bdrh[gg * 32:(gg + 1) * 32,
                                                 eb + gg * 32:eb + gg * 32 + 32],
                                            permt[gg * 32:(gg + 1) * 32,
                                                  tcx * 128 + j * 32:
                                                  tcx * 128 + j * 32 + 32]))
                                    wT(sv, ss)
                                    gsb = ss.n
                                    for dcx in range(DC):
                                        gslt = (dcx % 2) * 512
                                        if dcx >= 2:
                                            T_.wait_ge(ss.h, gsb + dcx - 1)
                                        MM(
                                            ps_a[:, gslt:gslt + 128],
                                            r1bk[:, tcx * 1024 + dcx * 128:
                                                 tcx * 1024 + (dcx + 1) * 128],
                                            bdrh[:, eb:eb + 128],
                                            start=True, stop=True)
                                        tmark()
                                        wC(st)
                                        cI(C_.copy(
                                            gth[:, dcx * 1024 + tcl * 128:
                                                dcx * 1024 + (tcl + 1) * 128],
                                            ps_a[:, gslt:gslt + 128]))
                                bar()
                                if debug and half == 0 and j == 0:
                                    dO(dbg3.ap(), gth[:, :])
                                    bar()
                                w1sb = ss.n
                                for e32 in range(32):
                                    es = j * 32 + e32
                                    rb = (e32 % 2) * 64
                                    PS = ps_a if e32 % 2 == 0 else ps_b
                                    if e32 >= 2:
                                        T_.wait_ge(ss.h, w1sb + e32 - 1)
                                    for dcx in range(DC):
                                        MM(
                                            PS[0:32, 0:32],
                                            w1b[:, dcx * 1024 + e32 * 32:
                                                dcx * 1024 + e32 * 32 + 32],
                                            gth[:, dcx * 1024:(dcx + 1) * 1024]
                                            .rearrange("p (tt e) -> p tt e",
                                                       e=32)[:, :, e32],
                                            start=(dcx == 0),
                                            stop=(dcx == DC - 1))
                                    tmark()
                                    wC(st)
                                    cI(C_.activation(
                                        innr[rb:rb + 32,
                                             (e32 // 2) * 32:
                                             (e32 // 2) * 32 + 32],
                                        PS[0:32, 0:32], ACT.Relu,
                                        bias=b1s[rb:rb + 32, es:es + 1],
                                        scale=1.0))
                                bar()
                                if debug and half == 0 and j == 0:
                                    dO(dbg4.ap(), innr[:, :])
                                    bar()
                                for pb_ in range(2):
                                    S_.wait_ge(st.h, st.n)
                                    dS(w2b[:, :].rearrange(
                                           "p (pr q) -> p pr q", pr=8),
                                       w2s_b.ap()[j * 2097152:(j + 1) * 2097152]
                                       .rearrange("(p pr q) -> p pr q",
                                                  p=128, q=1024)
                                       [:, pb_ * 8:(pb_ + 1) * 8, :])
                                    bar()
                                    w2sb = ss.n
                                    kk2 = 0
                                    for pr in range(8):
                                        for sub in range(2):
                                            e32 = (pb_ * 8 + pr) * 2 + sub
                                            es = j * 32 + e32
                                            rb = sub * 64
                                            for dcx in range(DC):
                                                PS = ps_a if kk2 % 2 == 0 else ps_b
                                                if kk2 >= 2:
                                                    T_.wait_ge(ss.h,
                                                               w2sb + kk2 - 1)
                                                MM(
                                                    PS[:, 0:32],
                                                    w2b[rb:rb + 32,
                                                        (pr * DC + dcx) * 128:
                                                        (pr * DC + dcx + 1) * 128],
                                                    innr[rb:rb + 32,
                                                         (e32 // 2) * 32:
                                                         (e32 // 2) * 32 + 32],
                                                    start=True, stop=True)
                                                tmark()
                                                wC(st)
                                                cI(C_.copy(
                                                    rpt[:, dcx * 4096:
                                                        (dcx + 1) * 4096]
                                                    .rearrange(
                                                        "p (n e) -> p n e",
                                                        e=128)[:, :, es],
                                                    PS[:, 0:32]))
                                                kk2 += 1
                                    bar()
                            if debug and half == 0:
                                dO(dbg5.ap(), rpt[:, :])
                                bar()
                            for gl in range(32):
                                gx = half * 32 + gl
                                tpb = ss.n
                                for dcx in range(DC):
                                    slt = (dcx % 2) * 1024
                                    if dcx >= 2:
                                        T_.wait_ge(ss.h, tpb + dcx - 1)
                                    TP(ps_t[:, slt:slt + 128],
                                       rpt[:, dcx * 4096 + gl * 128:
                                           dcx * 4096 + (gl + 1) * 128],
                                       identb[:, :])
                                    tmark()
                                    wC(st)
                                    cI(C_.copy(rpg[:, dcx * 128:(dcx + 1) * 128],
                                               ps_t[:, slt:slt + 128]))
                                wT(ss)
                                scb = sv.n
                                for dcx in range(DC):
                                    PS = ps_a if dcx % 2 == 0 else ps_b
                                    T_.wait_ge(sv.h, scb + dcx - 1)
                                    MM(PS[:, 0:32],
                                              rpg[:, dcx * 128:(dcx + 1) * 128],
                                              permb[:, gx * 32:(gx + 1) * 32],
                                              start=True, stop=False)
                                    MM(PS[:, 0:32],
                                              b2b[:, dcx * 128:(dcx + 1) * 128],
                                              permb[:, gx * 32:(gx + 1) * 32],
                                              start=False, stop=True)
                                    tmark()
                                    wV(st)
                                    V_.wait_ge(so.h, so.n - 16 * 7)
                                    vI(V_.tensor_copy(misc[:, (dcx % 8) * 32:
                                                           (dcx % 8) * 32 + 32],
                                                      PS[:, 0:32]))
                                    S_.wait_ge(sv.h, sv.n)
                                    dO(ffn_d.ap()[dcx * 128:(dcx + 1) * 128,
                                                  gx * 32:(gx + 1) * 32],
                                       misc[:, (dcx % 8) * 32:
                                            (dcx % 8) * 32 + 32])
                            bar()
                bar()

                _stk5b.close(); _stk5r.close(); _stk5.close()
                _stk6 = ExitStack()
                strm = _stk6.enter_context(nc.sbuf_tensor("strm2", [128, 16384], F32))
                _stkLN = ExitStack()
                sqt = _stkLN.enter_context(nc.sbuf_tensor("sqt2", [128, 1024], F32))
                sA = _stkLN.enter_context(nc.sbuf_tensor("sA2", [128, 2048], F32))
                sB = _stkLN.enter_context(nc.sbuf_tensor("sB2", [128, 2048], F32))
                ev32 = _stkLN.enter_context(nc.sbuf_tensor("ev322", [128, 2048], F32))
                for dcx in range(DC):
                    S_.wait_ge(sv.h, sv.n)
                    dS(strm[:, dcx * 2048:(dcx + 1) * 2048],
                       r1t_d.ap()[dcx * 128:(dcx + 1) * 128, :])
                    dS(ev32[:, 0:2048],
                       ffn_d.ap()[dcx * 128:(dcx + 1) * 128, :])
                    wV(sd)
                    vI(V_.tensor_tensor(strm[:, dcx * 2048:(dcx + 1) * 2048],
                                        strm[:, dcx * 2048:(dcx + 1) * 2048],
                                        ev32[:, 0:2048], ALU.add))
                bar()
                ln_fm(strm, 1e-5, sqt, sA, sB)
                if debug:
                    for dcx in range(DC):
                        dO(r2t_d.ap()[dcx * 128:(dcx + 1) * 128, :],
                           strm[:, dcx * 2048:(dcx + 1) * 2048])
                    bar()

                _stkLN.close()
                _stk7 = ExitStack()
                wt1 = _stk7.enter_context(nc.sbuf_tensor("wt1", [128, 8192], BF16))
                kpb = _stk7.enter_context(nc.sbuf_tensor("kpb", [128, 2048], BF16))
                vpb = _stk7.enter_context(nc.sbuf_tensor("vpb", [128, 2048], BF16))
                bkb = _stk7.enter_context(nc.sbuf_tensor("bkb", [128, 1024], F32))
                bvb = _stk7.enter_context(nc.sbuf_tensor("bvb", [128, 1024], F32))
                tq1 = _stk7.enter_context(nc.sbuf_tensor("tq1", [128, 1024], F32))
                tq2 = _stk7.enter_context(nc.sbuf_tensor("tq2", [128, 1024], F32))
                tq3 = _stk7.enter_context(nc.sbuf_tensor("tq3", [128, 1024], F32))
                _stk7r = ExitStack()
                r2b = _stk7r.enter_context(nc.sbuf_tensor("r2b", [128, 16384], BF16))
                if True:
                    def elu1(dst, src, tmp):
                        cI(C_.activation(tmp, src, ACT.Relu, scale=-1.0))
                        wC(ss)
                        cI(C_.activation(tmp, tmp, ACT.Exp, scale=-1.0))
                        vI(V_.tensor_scalar(out=tq3[:, 0:1024], in0=src,
                                            scalar1=0.0, scalar2=None,
                                            op0=ALU.max))
                        wV(ss, sv)
                        vI(V_.tensor_tensor(dst, tmp, tq3[:, 0:1024], ALU.add))

                    for dcx in range(DC):
                        vI(V_.tensor_copy(r2b[:, dcx * 2048:(dcx + 1) * 2048],
                                          strm[:, dcx * 2048:(dcx + 1) * 2048]))
                    dS(rowa[:, 0:1024], bk1r)
                    dS(rowb[:, 0:1024], bv1r)
                    bar()
                    for ntx in range(2):
                        MM(ps_a[:, ntx * 512:(ntx + 1) * 512],
                                  ones_f1[:, :],
                                  rowa[0:1, ntx * 512:(ntx + 1) * 512],
                                  start=True, stop=True)
                        MM(ps_b[:, ntx * 512:(ntx + 1) * 512],
                                  ones_f1[:, :],
                                  rowb[0:1, ntx * 512:(ntx + 1) * 512],
                                  start=True, stop=True)
                    tmark()
                    wC(st)
                    cI(C_.copy(bkb[:, :], ps_a[:, 0:1024]))
                    cI(C_.copy(bvb[:, :], ps_b[:, 0:1024]))
                    for b in range(B):
                        S_.wait_ge(st.h, st.n)
                        for dcx in range(DC):
                            dS(wt1[:, dcx * 1024:(dcx + 1) * 1024],
                               wk1_b.ap()[dcx * 131072:(dcx + 1) * 131072]
                               .rearrange("(p q) -> p q", p=128))
                        bar()
                        for i2 in range(2):
                            tcx = b * 2 + i2
                            PS = ps_a if i2 % 2 == 0 else ps_b
                            wT(ss, sv) if i2 == 0 else None
                            for dcx in range(DC):
                                for nh in range(2):
                                    MM(
                                        PS[:, nh * 512:nh * 512 + 512],
                                        r2b[:, dcx * 2048 + tcx * 128:
                                            dcx * 2048 + (tcx + 1) * 128],
                                        wt1[:, dcx * 1024 + nh * 512:
                                            dcx * 1024 + nh * 512 + 512],
                                        start=(dcx == 0),
                                        stop=(dcx == DC - 1))
                            tmark()
                            wV(st, ss)
                            vI(V_.tensor_tensor(tq1[:, :], PS[:, 0:1024],
                                                bkb[:, :], ALU.add))
                            wC(sv)
                            elu1(kpb[:, i2 * 1024:(i2 + 1) * 1024], tq1[:, :],
                                 tq2[:, :])
                        S_.wait_ge(st.h, st.n)
                        for dcx in range(DC):
                            dS(wt1[:, dcx * 1024:(dcx + 1) * 1024],
                               wv1_b.ap()[dcx * 131072:(dcx + 1) * 131072]
                               .rearrange("(p q) -> p q", p=128))
                        bar()
                        for i2 in range(2):
                            tcx = b * 2 + i2
                            wT(ss, sv)
                            for dcx in range(DC):
                                for nh in range(2):
                                    MM(
                                        ps_a[:, nh * 512:nh * 512 + 512],
                                        r2b[:, dcx * 2048 + tcx * 128:
                                            dcx * 2048 + (tcx + 1) * 128],
                                        wt1[:, dcx * 1024 + nh * 512:
                                            dcx * 1024 + nh * 512 + 512],
                                        start=(dcx == 0),
                                        stop=(dcx == DC - 1))
                            tmark()
                            wV(st)
                            vI(V_.tensor_tensor(vpb[:, i2 * 1024:
                                                    (i2 + 1) * 1024],
                                                ps_a[:, 0:1024], bvb[:, :],
                                                ALU.add))
                        V_.wait_ge(sd.h, sd.n)
                        wT(sv, ss)
                        for h in range(H):
                            slt = (h % 2) * 512
                            wT(sv)
                            for i2 in range(2):
                                MM(ps_b[0:64, slt:slt + 64],
                                          kpb[:, i2 * 1024 + h * 64:
                                              i2 * 1024 + h * 64 + 64],
                                          vpb[:, i2 * 1024 + h * 64:
                                              i2 * 1024 + h * 64 + 64],
                                          start=(i2 == 0), stop=(i2 == 1))
                            tmark()
                            wV(st)
                            vI(V_.tensor_copy(tq2[0:64, h * 64:(h + 1) * 64],
                                              ps_b[0:64, slt:slt + 64]))
                        S_.wait_ge(sv.h, sv.n)
                        dS(kvs_in.ap()[b * H * HD * HD:(b + 1) * H * HD * HD]
                           .rearrange("(h p q) -> p h q", p=64, q=64),
                           tq2[0:64, 0:1024]
                           .rearrange("p (h q) -> p h q", q=64))
                    bar()
                    G_.wait_ge(sg.h, sg.n)
                    G_.wait_ge(sgd.h, sgd.n)
                    G_.collective_compute(
                        "AllReduce", ALU.add,
                        replica_groups=[list(range(NCORE))],
                        ins=[kvs_in.ap()], outs=[kvs_out.ap()]).then_inc(sc.h, 1)
                    sc.n += 1
                    bar()
                    for dcx in range(DC):
                        dS(wt1[:, dcx * 1024:(dcx + 1) * 1024],
                           wq1_b.ap()[dcx * 131072:(dcx + 1) * 131072]
                           .rearrange("(p q) -> p q", p=128))
                    bar()
                    _stk7q = ExitStack()
                    qp = _stk7q.enter_context(nc.sbuf_tensor("qp", [128, 16384], BF16))
                    qpb = ss.n
                    kq = 0
                    for mcx in range(DC):
                        for ntx in range(2):
                            PS = ps_a if kq % 2 == 0 else ps_b
                            if kq >= 2:
                                T_.wait_ge(ss.h, qpb + 3 * (kq - 1))
                            for dcx in range(DC):
                                for nh in range(2):
                                    MM(
                                        PS[:, nh * 512:nh * 512 + 512],
                                        wt1[:, dcx * 1024 + mcx * 128:
                                            dcx * 1024 + (mcx + 1) * 128],
                                        r2b[:, dcx * 2048 + ntx * 1024 + nh * 512:
                                            dcx * 2048 + ntx * 1024 + nh * 512 + 512],
                                        start=(dcx == 0),
                                        stop=(dcx == DC - 1))
                            tmark()
                            wC(st, sv)
                            cI(C_.activation(tq1[:, :], PS[:, 0:1024],
                                             ACT.Identity,
                                             bias=bq1s[:, mcx:mcx + 1],
                                             scale=1.0))
                            wV(ss)
                            elu1(qp[:, mcx * 2048 + ntx * 1024:
                                    mcx * 2048 + ntx * 1024 + 1024],
                                 tq1[:, :], tq2[:, :])
                            kq += 1
                    bar()
                    dG(wt1[0:64, 0:8192].rearrange(
                           "p (b h q) -> p b h q", b=B, h=H),
                       kvs_out.ap().rearrange("(b h p q) -> p b h q",
                                              p=HD, q=HD, h=H))
                    dG(wt1[64:128, 0:8192].rearrange(
                           "p (b h q) -> p b h q", b=B, h=H),
                       kvs_out.ap().rearrange("(b h p q) -> p b h q",
                                              p=HD, q=HD, h=H))
                    bar()
                    for b in range(B):
                        for h in range(H):
                            rq, mq = (h % 2) * 64, h // 2
                            C_.wait_ge(so.h, so.n)
                            wC(st)
                            wT(ss)
                            MM(ps_s[0:64, 0:256],
                                      wt1[rq:rq + 64, (b * H + h) * 64:
                                          (b * H + h + 1) * 64],
                                      qp[rq:rq + 64, mq * 2048 + b * 256:
                                         mq * 2048 + (b + 1) * 256],
                                      start=True, stop=True)
                            tmark()
                            wC(st)
                            cI(C_.copy(kpb[0:64, 0:256], ps_s[0:64, 0:256]))
                            S_.wait_ge(ss.h, ss.n)
                            dO(at1_d.ap()[mq * 256 + rq:mq * 256 + rq + 64,
                                          b * 256:(b + 1) * 256]
                               if False else
                               at1_d.ap()[(h // 2) * 128 + (h % 2) * 64:
                                          (h // 2) * 128 + (h % 2) * 64 + 64,
                                          b * 256:(b + 1) * 256],
                               kpb[0:64, 0:256])
                    bar()
                    _stk7q.close()
                    _stk7r.close()
                    for dcx in range(DC):
                        dS(wt1[:, dcx * 1024:(dcx + 1) * 1024],
                           wp1_b.ap()[dcx * 131072:(dcx + 1) * 131072]
                           .rearrange("(p q) -> p q", p=128))
                    bar()
                    for mcx in range(DC):
                        for ntx in range(2):
                            wT(ss, sv)
                            for dcx in range(DC):
                                hb = (dcx % 2) * 1024
                                G_.wait_ge(st.h, st.n)
                                dG(kpb[:, hb:hb + 1024],
                                   at1_d.ap()[dcx * 128:(dcx + 1) * 128,
                                              ntx * 1024:(ntx + 1) * 1024])
                                wT(sg)
                                for nh in range(2):
                                    MM(
                                        ps_a[:, nh * 512:nh * 512 + 512],
                                        wt1[:, dcx * 1024 + mcx * 128:
                                            dcx * 1024 + (mcx + 1) * 128],
                                        kpb[:, hb + nh * 512:
                                            hb + nh * 512 + 512],
                                        start=(dcx == 0),
                                        stop=(dcx == DC - 1))
                                tmark()
                            wC(st)
                            cI(C_.activation(tq1[:, :], ps_a[:, 0:1024],
                                             ACT.Identity,
                                             bias=bp1s[:, mcx:mcx + 1],
                                             scale=1.0))
                            wV(ss)
                            vI(V_.tensor_tensor(
                                strm[:, mcx * 2048 + ntx * 1024:
                                     mcx * 2048 + ntx * 1024 + 1024],
                                strm[:, mcx * 2048 + ntx * 1024:
                                     mcx * 2048 + ntx * 1024 + 1024],
                                tq1[:, :], ALU.add))
                    bar()
                _stk7.close()
                _stkLN = ExitStack()
                sqt = _stkLN.enter_context(nc.sbuf_tensor("sqt3", [128, 1024], F32))
                sA = _stkLN.enter_context(nc.sbuf_tensor("sA3", [128, 2048], F32))
                sB = _stkLN.enter_context(nc.sbuf_tensor("sB3", [128, 2048], F32))
                ev32 = _stkLN.enter_context(nc.sbuf_tensor("ev323", [128, 2048], F32))
                ln_fm(strm, 1e-5, sqt, sA, sB)
                if debug:
                    for dcx in range(DC):
                        dO(r3t_d.ap()[dcx * 128:(dcx + 1) * 128, :],
                           strm[:, dcx * 2048:(dcx + 1) * 2048])
                    bar()

                _stkLN.close()
                _stk8 = ExitStack()
                xq1 = _stk8.enter_context(nc.sbuf_tensor("xq1", [128, 16384], BF16))
                wqb = _stk8.enter_context(nc.sbuf_tensor("wqb", [128, 4096], BF16))
                accm = _stk8.enter_context(nc.sbuf_tensor("accm", [128, 2048], F32))
                accn = _stk8.enter_context(nc.sbuf_tensor("accn", [128, 2048], F32))
                s1b = _stk8.enter_context(nc.sbuf_tensor("s1b", [128, 2048], F32))
                rs1b = _stk8.enter_context(nc.sbuf_tensor("rs1b", [128, 2048], F32))
                halT = _stk8.enter_context(nc.sbuf_tensor("halT", [128, 512], BF16))
                sw2s = _stk8.enter_context(nc.sbuf_tensor("sw2s", [128, 2], F32))
                _stk8b = ExitStack()
                wtf = _stk8b.enter_context(nc.sbuf_tensor("wtf", [128, 4096], F32))
                heg = _stk8b.enter_context(nc.sbuf_tensor("heg", [16, 4096], BF16))
                xq1e = _stk8b.enter_context(nc.sbuf_tensor("xq1e", [128, 128], BF16))
                if True:
                    def token_stats(src, cols, nblk):
                        wV(ss, st)
                        vI(V_.memset(misc[:, 0:256], 0.0))
                        vI(V_.memset(misc2[:, 0:256], 0.0))
                        """per-token max over partitions of src[:, 0:cols];
                        rowa <- 127/max(m,1e-5); rowb <- max(m,1e-5)/127
                        (flattened to [1, cols])."""
                        for i in range(nblk):
                            T_.wait_ge(sv.h, sv.n)
                            TP(ps_a[:, 0:128],
                                         src[:, i * 128:(i + 1) * 128],
                                         identf[:, :])
                            tmark()
                            wV(st)
                            vI(V_.tensor_reduce(misc[:, i:i + 1],
                                                ps_a[:, 0:128], AX, ALU.max))
                        wV(sv)
                        vI(V_.tensor_scalar(out=misc[:, 0:nblk],
                                            in0=misc[:, 0:nblk], scalar1=1e-5,
                                            scalar2=None, op0=ALU.max))
                        wV(sv)
                        vI(V_.tensor_scalar(out=misc2[:, 0:nblk],
                                            in0=misc[:, 0:nblk],
                                            scalar1=1.0 / 127.0, scalar2=None,
                                            op0=ALU.mult))
                        vI(V_.reciprocal(misc[:, 0:nblk], misc[:, 0:nblk]))
                        wV(sv)
                        vI(V_.tensor_scalar(out=misc[:, 0:nblk],
                                            in0=misc[:, 0:nblk], scalar1=127.0,
                                            scalar2=None, op0=ALU.mult))
                        wT(sv)
                        TP(ps_a[:, 0:128], misc[:, 0:128], identf[:, :])
                        TP(ps_b[:, 0:128], misc2[:, 0:128],
                                     identf[:, :])
                        tmark()
                        wC(st)
                        cI(C_.copy(misc[:, 128:256], ps_a[:, 0:128]))
                        cI(C_.copy(misc2[:, 128:256], ps_b[:, 0:128]))
                        S_.wait_ge(ss.h, ss.n)
                        dS(row_d.ap()[0:cols].rearrange("(a q) -> a q", a=nblk),
                           misc[0:nblk, 128:256])
                        dS(row_d.ap()[2048:2048 + cols]
                           .rearrange("(a q) -> a q", a=nblk),
                           misc2[0:nblk, 128:256])
                        bar()
                        dS(rowa[:, 0:cols],
                           row_d.ap()[0:cols].rearrange("(o q) -> o q", o=1))
                        dS(rowb[:, 0:cols],
                           row_d.ap()[2048:2048 + cols]
                           .rearrange("(o q) -> o q", o=1))
                        bar()

                    def bcast2(dstA, dstB, ncols):
                        wT(sv, ss)
                        for blk in range(ncols // 512):
                            c0 = blk * 512
                            wT(ss)
                            MM(ps_a[:, 0:512], ones_f1[:, :],
                               rowa[0:1, c0:c0 + 512], start=True, stop=True)
                            MM(ps_b[:, 0:512], ones_f1[:, :],
                               rowb[0:1, c0:c0 + 512], start=True, stop=True)
                            tmark()
                            wC(st)
                            cI(C_.copy(dstA[:, c0:c0 + 512], ps_a[:, 0:512]))
                            cI(C_.copy(dstB[:, c0:c0 + 512], ps_b[:, 0:512]))
                        bar()

                    vI(V_.tensor_copy(accm[:, :], strm[:, 0:2048]))
                    vI(V_.tensor_copy(accn[:, :], strm[:, 0:2048]))
                    wV(sv)
                    for dcx in range(1, DC):
                        vI(V_.tensor_tensor(accm[:, :], accm[:, :],
                                            strm[:, dcx * 2048:(dcx + 1) * 2048],
                                            ALU.max))
                        vI(V_.tensor_tensor(accn[:, :], accn[:, :],
                                            strm[:, dcx * 2048:(dcx + 1) * 2048],
                                            ALU.min))
                    wV(sv)
                    vI(V_.tensor_scalar(out=accn[:, :], in0=accn[:, :],
                                        scalar1=-1.0, scalar2=None, op0=ALU.mult))
                    wV(sv)
                    vI(V_.tensor_tensor(accm[:, :], accm[:, :], accn[:, :],
                                        ALU.max))
                    bar()
                    token_stats(accm, 2048, 16)
                    bcast2(s1b, rs1b, 2048)
                    for dcx in range(DC):
                        wV(ss)
                        vI(V_.tensor_tensor(accm[:, :],
                                            strm[:, dcx * 2048:(dcx + 1) * 2048],
                                            s1b[:, :], ALU.mult))
                        wV(sv)
                        vI(V_.tensor_scalar(
                            out=xq1[:, dcx * 2048:(dcx + 1) * 2048],
                            in0=accm[:, :], scalar1=RNE, scalar2=RNE,
                            op0=ALU.add, op1=ALU.subtract))
                    if debug:
                        S_.wait_ge(sv.h, sv.n)
                        S_.wait_ge(ss.h, ss.n)
                        dO(dbg1.ap(), s1b[:, :])
                        dO(dbg5.ap()[0:128, 0:16384], xq1[:, :])
                    bar()
                    vI(V_.memset(misc[:, 0:1], 0.0))
                    for dcx in range(DC):
                        dS(wtf[:, :], w1t[dcx * 128:(dcx + 1) * 128, :])
                        wV(sd, sv)
                        vI(V_.tensor_reduce(misc2[:, 0:1], wtf[:, :], AX,
                                            ALU.add, apply_absolute_value=True))
                        wV(sv)
                        vI(V_.tensor_tensor(misc[:, 0:1], misc[:, 0:1],
                                            misc2[:, 0:1], ALU.add))
                    wT(sv)
                    MM(ps_s[0:1, 0:1], ones_ff[:, :], misc[:, 0:1],
                              start=True, stop=True)
                    tmark()
                    wV(st)
                    vI(V_.tensor_scalar(out=rowa[0:1, 0:1], in0=ps_s[0:1, 0:1],
                                        scalar1=1.0 / (D * DH), scalar2=1e-5,
                                        op0=ALU.mult, op1=ALU.max))
                    wV(sv)
                    vI(V_.reciprocal(rowa[0:1, 1:2], rowa[0:1, 0:1]))
                    wT(sv)
                    MM(ps_s[:, 0:2], ones_f1[:, :], rowa[0:1, 0:2],
                              start=True, stop=True)
                    tmark()
                    wC(st)
                    cI(C_.copy(misc[:, 0:2], ps_s[:, 0:2]))
                    bar()
                    for dcx in range(DC):
                        dS(wtf[:, :], w1t[dcx * 128:(dcx + 1) * 128, :])
                        wV(sd, ss)
                        vI(V_.tensor_scalar(out=wtf[:, :], in0=wtf[:, :],
                                            scalar1=misc[:, 1:2], scalar2=RNE,
                                            op0=ALU.mult, op1=ALU.add))
                        wV(sv)
                        vI(V_.tensor_scalar(out=wtf[:, :], in0=wtf[:, :],
                                            scalar1=RNE, scalar2=1.0,
                                            op0=ALU.subtract, op1=ALU.min))
                        wV(sv)
                        vI(V_.tensor_scalar(out=wqb[:, :], in0=wtf[:, :],
                                            scalar1=-1.0, scalar2=None,
                                            op0=ALU.max))
                        S_.wait_ge(sv.h, sv.n)
                        dS(w1q_d.ap()[dcx], wqb[:, :])
                    for half in range(2):
                        wV(ss, sd)
                        vI(V_.tensor_scalar(
                            out=rs1b[:, half * 1024:half * 1024 + 1024],
                            in0=rs1b[:, half * 1024:half * 1024 + 1024],
                            scalar1=misc[:, 0:1], scalar2=None, op0=ALU.mult))
                    bar()
                    wV(sd)
                    for dcx in range(DC):
                        for e2, off in ((0, 0), (1, 255)):
                            vI(V_.tensor_copy(
                                xq1e[:, dcx * 16:(dcx + 1) * 16]
                                .rearrange("p (bb e) -> p bb e", e=2)[:, :, e2],
                                xq1[:, dcx * 2048:(dcx + 1) * 2048]
                                .rearrange("p (bb q) -> p bb q", q=256)
                                [:, :, off]))
                    dS(misc2[0:16:2, 2:3],
                       row_d.ap()[2048:4096:256][:, None], slow=True)
                    dS(misc2[1:16:2, 2:3],
                       row_d.ap()[2048 + 255:4096:256][:, None], slow=True)
                    bar()
                    vI(V_.tensor_scalar(out=misc2[0:16, 2:3],
                                        in0=misc2[0:16, 2:3],
                                        scalar1=misc[0:16, 0:1], scalar2=None,
                                        op0=ALU.mult))
                    bar()
                    for nb in range(8):
                        wC(sv)
                        for dcx in range(DC):
                            S_.wait_ge(st.h, st.n)
                            dS(wqb[:, 0:512],
                               w1q_d.ap()[dcx, :, nb * 512:(nb + 1) * 512])
                            wT(sd)
                            wT(ss)
                            MM(ps_s[0:16, 0:512],
                                      xq1e[:, dcx * 16:(dcx + 1) * 16],
                                      wqb[:, 0:512],
                                      start=(dcx == 0), stop=(dcx == DC - 1))
                            tmark()
                        wC(st)
                        cI(C_.activation(heg[0:16, nb * 512:(nb + 1) * 512],
                                         ps_s[0:16, 0:512], ACT.Identity,
                                         bias=0.0, scale=misc2[0:16, 2:3]))
                    bar()
                    dS(halo_in.ap(), heg[0:16, :])
                    bar()
                    G_.wait_ge(sg.h, sg.n)
                    G_.wait_ge(sgd.h, sgd.n)
                    G_.collective_compute(
                        "AllGather", ALU.bypass,
                        replica_groups=[list(range(NCORE))],
                        ins=[halo_in.ap().rearrange("a b -> (a b)")],
                        outs=[halo_out.ap().rearrange("a b -> (a b)")]
                    ).then_inc(sc.h, 1)
                    sc.n += 1
                    bar()
                    dS(wqb[:, 0:4096], halo_out.ap())
                    bar()
                    for nb in range(8):
                        wT(ss)
                        MM(ps_s[0:16, 0:512], hsels[:, :],
                                  wqb[:, nb * 512:(nb + 1) * 512],
                                  start=True, stop=True)
                        tmark()
                        wC(st)
                        cI(C_.copy(heg[0:16, nb * 512:(nb + 1) * 512],
                                   ps_s[0:16, 0:512]))
                    bar()
                    for hcx in range(32):
                        slt = (hcx % 2) * 1024
                        T_.wait_ge(ss.h, ss.n)
                        TP(ps_t[:, slt:slt + 16],
                                     heg[0:16, hcx * 128:(hcx + 1) * 128],
                                     identb[0:16, 0:16])
                        tmark()
                        wC(st)
                        cI(C_.copy(halT[:, hcx * 16:(hcx + 1) * 16],
                                   ps_t[:, slt:slt + 16]))
                    bar()
                    vI(V_.memset(misc[:, 0:1], 0.0))
                    for hcx in range(32):
                        dS(wtf[:, 0:1024], w2t[hcx * 128:(hcx + 1) * 128, :])
                        wV(sd, sv)
                        vI(V_.tensor_reduce(misc2[:, 0:1], wtf[:, 0:1024], AX,
                                            ALU.add, apply_absolute_value=True))
                        wV(sv)
                        vI(V_.tensor_tensor(misc[:, 0:1], misc[:, 0:1],
                                            misc2[:, 0:1], ALU.add))
                    wT(sv)
                    MM(ps_s[0:1, 0:1], ones_ff[:, :], misc[:, 0:1],
                              start=True, stop=True)
                    tmark()
                    wV(st)
                    vI(V_.tensor_scalar(out=rowa[0:1, 0:1], in0=ps_s[0:1, 0:1],
                                        scalar1=1.0 / (D * DH), scalar2=1e-5,
                                        op0=ALU.mult, op1=ALU.max))
                    wV(sv)
                    vI(V_.reciprocal(rowa[0:1, 1:2], rowa[0:1, 0:1]))
                    wT(sv)
                    MM(ps_s[:, 0:2], ones_f1[:, :], rowa[0:1, 0:2],
                              start=True, stop=True)
                    tmark()
                    wC(st)
                    cI(C_.copy(misc[:, 0:2], ps_s[:, 0:2]))
                    bar()
                    for hcx in range(32):
                        dS(wtf[:, 0:1024], w2t[hcx * 128:(hcx + 1) * 128, :])
                        wV(sd, ss)
                        vI(V_.tensor_scalar(out=wtf[:, 0:1024],
                                            in0=wtf[:, 0:1024],
                                            scalar1=misc[:, 1:2], scalar2=RNE,
                                            op0=ALU.mult, op1=ALU.add))
                        wV(sv)
                        vI(V_.tensor_scalar(out=wtf[:, 0:1024],
                                            in0=wtf[:, 0:1024],
                                            scalar1=RNE, scalar2=1.0,
                                            op0=ALU.subtract, op1=ALU.min))
                        wV(sv)
                        vI(V_.tensor_scalar(out=wqb[:, 0:1024],
                                            in0=wtf[:, 0:1024],
                                            scalar1=-1.0, scalar2=None,
                                            op0=ALU.max))
                        S_.wait_ge(sv.h, sv.n)
                        dS(w2q_d.ap()[hcx], wqb[:, 0:1024])
                    bar()
                    vI(V_.tensor_copy(sw2s[:, 0:2], misc[:, 0:2]))
                    bar()

                    _stk8b.close()
                    _stk8c = ExitStack()
                    hq = _stk8c.enter_context(nc.sbuf_tensor("hq", [128, 16384], BF16))
                    for tq in range(4):
                        tc0 = tq * 512
                        hvb = sv.n
                        dS(wqb[:, 0:1024]
                           .rearrange("p (a q) -> p a q", a=DC),
                           w1q_d.ap()[:, :, 0:128]
                           .rearrange("a p q -> p a q"))
                        pfh = {0: sd.n}
                        for hcx in range(32):
                            PS = ps_a if hcx % 2 == 0 else ps_b
                            if hcx + 1 < 32:
                                S_.wait_ge(st.h, st.n)
                                dS(wqb[:, ((hcx + 1) % 2) * 1024:
                                       ((hcx + 1) % 2) * 1024 + 1024]
                                   .rearrange("p (a q) -> p a q", a=DC),
                                   w1q_d.ap()[:, :, (hcx + 1) * 128:
                                              (hcx + 2) * 128]
                                   .rearrange("a p q -> p a q"))
                                pfh[hcx + 1] = sd.n
                            T_.wait_ge(sd.h, pfh[hcx])
                            if hcx >= 2:
                                T_.wait_ge(sv.h, hvb + hcx - 1)
                            for dcx in range(DC):
                                MM(PS[:, 0:512],
                                          wqb[:, (hcx % 2) * 1024 + dcx * 128:
                                              (hcx % 2) * 1024 + (dcx + 1) * 128],
                                          xq1[:, dcx * 2048 + tc0:
                                              dcx * 2048 + tc0 + 512],
                                          start=(dcx == 0),
                                          stop=(dcx == DC - 1))
                            tmark()
                            wV(st)
                            vI(V_.tensor_tensor(
                                hq[:, hcx * 512:(hcx + 1) * 512],
                                PS[:, 0:512], rs1b[:, tc0:tc0 + 512],
                                ALU.mult))
                        if debug and tq == 0:
                            S_.wait_ge(sv.h, sv.n)
                            S_.wait_ge(ss.h, ss.n)
                            dO(dbg3.ap(), hq[:, 0:8192])
                            dO(dbg2.ap(), rs1b[:, :])
                            bar()
                        bar()
                        hv = hq[:, 0:16384].rearrange("p (hh q) -> p hh q",
                                                      q=512)
                        w0b = cvws[:, :, 0][:, :, None]
                        w1b_ = cvws[:, :, 1][:, :, None]
                        w2b_ = cvws[:, :, 2][:, :, None]
                        for hq4 in range(8):
                            h0 = hq4 * 4
                            src = hv[:, h0:h0 + 4, :]
                            a4 = accm[:, 0:2048].rearrange(
                                "p (hh q) -> p hh q", q=512)
                            t4 = accn[:, 0:2048].rearrange(
                                "p (hh q) -> p hh q", q=512)
                            wV(ss, sv)
                            vI(V_.tensor_tensor(
                                a4, src, w1b_[:, h0:h0 + 4, :]
                                .broadcast_to([128, 4, 512]), ALU.mult))
                            vI(V_.tensor_tensor(
                                t4[:, :, 1:512], src[:, :, 0:511],
                                w0b[:, h0:h0 + 4, :]
                                .broadcast_to([128, 4, 511]), ALU.mult))
                            wV(sv)
                            vI(V_.tensor_tensor(a4[:, :, 1:512],
                                                a4[:, :, 1:512],
                                                t4[:, :, 1:512], ALU.add))
                            vI(V_.tensor_tensor(
                                t4[:, :, 0:511], src[:, :, 1:512],
                                w2b_[:, h0:h0 + 4, :]
                                .broadcast_to([128, 4, 511]), ALU.mult))
                            wV(sv)
                            vI(V_.tensor_tensor(a4[:, :, 0:511],
                                                a4[:, :, 0:511],
                                                t4[:, :, 0:511], ALU.add))
                            b0_, b1_ = 2 * tq, 2 * tq + 1
                            hs = halT[:, :].rearrange("p (hh s) -> p hh s",
                                                      s=16)
                            for (col, tap, bb, dirr, sub_col) in (
                                    (0, w0b, b0_, 0, None),
                                    (256, w0b, b1_, 0, 255),
                                    (255, w2b_, b0_, 1, 256),
                                    (511, w2b_, b1_, 1, None)):
                                wV(sv)
                                vI(V_.tensor_tensor(
                                    t4[:, :, col:col + 1],
                                    hs[:, h0:h0 + 4, bb * 2 + dirr:
                                       bb * 2 + dirr + 1],
                                    tap[:, h0:h0 + 4, :]
                                    .broadcast_to([128, 4, 1]), ALU.mult))
                                wV(sv)
                                vI(V_.tensor_tensor(a4[:, :, col:col + 1],
                                                    a4[:, :, col:col + 1],
                                                    t4[:, :, col:col + 1],
                                                    ALU.add))
                                if sub_col is not None:
                                    vI(V_.tensor_tensor(
                                        t4[:, :, col:col + 1],
                                        src[:, :, sub_col:sub_col + 1],
                                        tap[:, h0:h0 + 4, :]
                                        .broadcast_to([128, 4, 1]), ALU.mult))
                                    wV(sv)
                                    vI(V_.tensor_scalar(
                                        out=t4[:, :, col:col + 1],
                                        in0=t4[:, :, col:col + 1],
                                        scalar1=-1.0, scalar2=None,
                                        op0=ALU.mult))
                                    wV(sv)
                                    vI(V_.tensor_tensor(a4[:, :, col:col + 1],
                                                        a4[:, :, col:col + 1],
                                                        t4[:, :, col:col + 1],
                                                        ALU.add))
                            for hh in range(4):
                                wC(sv)
                                cI(C_.activation(
                                    hv[:, h0 + hh, :],
                                    a4[:, hh, :], ACT.Silu,
                                    bias=cvbs[:, h0 + hh:h0 + hh + 1],
                                    scale=1.0))
                        if debug and tq == 0:
                            S_.wait_ge(ss.h, ss.n)
                            S_.wait_ge(sv.h, sv.n)
                            dO(dbg5.ap()[0:128, 16384:24576], hq[:, 0:8192])
                            bar()
                        bar()
                        wC(ss)
                        vI(V_.tensor_copy(accm[:, 0:512], hv[:, 0, :]))
                        vI(V_.tensor_copy(accn[:, 0:512], hv[:, 0, :]))
                        wV(sv)
                        for hcx in range(1, 32):
                            vI(V_.tensor_tensor(accm[:, 0:512], accm[:, 0:512],
                                                hv[:, hcx, :], ALU.max))
                            vI(V_.tensor_tensor(accn[:, 0:512], accn[:, 0:512],
                                                hv[:, hcx, :], ALU.min))
                        wV(sv)
                        vI(V_.tensor_scalar(out=accn[:, 0:512],
                                            in0=accn[:, 0:512], scalar1=-1.0,
                                            scalar2=None, op0=ALU.mult))
                        wV(sv)
                        vI(V_.tensor_tensor(accm[:, 0:512], accm[:, 0:512],
                                            accn[:, 0:512], ALU.max))
                        bar()
                        token_stats(accm, 512, 4)
                        bcast2(s1b[:, 0:512], accn[:, 0:512], 512)
                        for hcx in range(32):
                            wV(ss, sv)
                            vI(V_.tensor_tensor(accm[:, 0:512], hv[:, hcx, :],
                                                s1b[:, 0:512], ALU.mult))
                            wV(sv)
                            vI(V_.tensor_scalar(out=hv[:, hcx, :],
                                                in0=accm[:, 0:512],
                                                scalar1=RNE, scalar2=RNE,
                                                op0=ALU.add, op1=ALU.subtract))
                        if debug and tq == 0:
                            S_.wait_ge(sv.h, sv.n)
                            dO(dbg5.ap()[0:128, 24576:32768], hq[:, 0:8192])
                            bar()
                        bar()
                        w2vb = sv.n
                        for mcx in range(DC):
                            PS = ps_a if mcx % 2 == 0 else ps_b
                            S_.wait_ge(st.h, st.n)
                            dS(wqb[:, 0:4096].rearrange(
                                   "p (a q) -> p a q", a=32),
                               w2q_d.ap()[:, :, mcx * 128:(mcx + 1) * 128]
                               .rearrange("a p q -> p a q"))
                            wT(sd)
                            if mcx >= 2:
                                T_.wait_ge(sv.h, w2vb + 2 * (mcx - 1))
                            for hcx in range(32):
                                MM(PS[:, 0:512],
                                          wqb[:, hcx * 128:(hcx + 1) * 128],
                                          hv[:, hcx, :],
                                          start=(hcx == 0), stop=(hcx == 31))
                            tmark()
                            wC(st, sv)
                            cI(C_.activation(accm[:, 0:512], PS[:, 0:512],
                                             ACT.Identity, bias=0.0,
                                             scale=sw2s[:, 0:1]))
                            wV(ss)
                            vI(V_.tensor_tensor(accm[:, 0:512], accm[:, 0:512],
                                                accn[:, 0:512], ALU.mult))
                            wV(sv)
                            vI(V_.tensor_tensor(
                                strm[:, mcx * 2048 + tc0:
                                     mcx * 2048 + tc0 + 512],
                                strm[:, mcx * 2048 + tc0:
                                     mcx * 2048 + tc0 + 512],
                                accm[:, 0:512], ALU.add))
                        bar()
                _stk8c.close()
                _stk8.close()
                _stkF = ExitStack()
                sqt = _stkF.enter_context(nc.sbuf_tensor("sqtF", [128, 1024], F32))
                sA = _stkF.enter_context(nc.sbuf_tensor("sAF", [128, 2048], F32))
                sB = _stkF.enter_context(nc.sbuf_tensor("sBF", [128, 2048], F32))
                bar()
                ln_fm(strm, 1e-5, sqt, sA, sB)
                for dcx in range(DC):
                    dO(out_own[dcx * 128:(dcx + 1) * 128, :],
                       strm[:, dcx * 2048:(dcx + 1) * 2048])
                bar()
                _stkF.close()
            _stk6.close()
    _top.close()
    for cm in reversed(persist):
        cm.__exit__(None, None, None)
    return nc


def _stage(core, x, W):
    sl = slice(core * SO, (core + 1) * SO)
    d = {}
    d["xt"] = np.ascontiguousarray(
        x[:, sl, :].transpose(2, 0, 1).reshape(D, T))
    relp = np.concatenate(
        [W["rel"], np.tile(W["rel"][2046:2047], (2048, 1))], 0)
    hh = np.arange(H)
    ii = np.arange(S)[None, None, :]
    jj = np.arange(S)[None, :, None]
    iq = np.arange(core * SO, (core + 1) * SO)
    idx = ((iq[None, None, :] % 16) * 128 + jj // 16
           - (hh[:, None, None] * 128 + iq[None, None, :] // 16) + 2047)
    d["bt"] = np.ascontiguousarray(
        relp[idx, (jj % 16)].astype(np.float32))
    d["wq0"], d["wk0"], d["wv0"] = W["Wq0"], W["Wk0"], W["Wv0"]
    d["g0"] = np.ascontiguousarray(W["g0"].reshape(DC, 128).T)
    d["b0"] = np.ascontiguousarray(W["b0"].reshape(DC, 128).T)
    d["wc"] = np.ascontiguousarray(W["Wc"].reshape(D, ES))
    d["bcb"] = np.ascontiguousarray(W["bc"].reshape(ES, 1))
    W1 = W["W1"].reshape(ES, D, FF)
    d["w1s"] = np.ascontiguousarray(
        W1.reshape(4, 32, DC, 128, FF).transpose(0, 2, 3, 1, 4))
    b1 = W["b1"].reshape(ES, FF)
    b1p = np.zeros((128, ES), np.float32)
    b1p[0:32, :] = b1.T
    b1p[64:96, :] = b1.T
    d["b1p"] = b1p
    W2 = W["W2"].reshape(ES, D, FF)
    w2s = np.zeros((4, 128, 16, DC, 128), np.float32)
    for j in range(4):
        for pair in range(16):
            for sub in range(2):
                es = j * 32 + pair * 2 + sub
                wt = W2[es].reshape(DC, 128, FF)
                w2s[j, sub * 64:sub * 64 + 32, pair] = wt.transpose(2, 0, 1)
    d["w2s"] = w2s
    d["b2p"] = np.ascontiguousarray(W["b2"].reshape(ES, D))
    d["wq1"], d["wk1"], d["wv1"], d["wp1"] = (W["Wq1"], W["Wk1"], W["Wv1"],
                                              W["Wp1"])
    d["bq1"] = np.ascontiguousarray(W["bq1"].reshape(DC, 128).T)
    d["bp1"] = np.ascontiguousarray(W["bp1"].reshape(DC, 128).T)
    d["bk1r"] = W["bk1"].reshape(1, D)
    d["bv1r"] = W["bv1"].reshape(1, D)
    d["w1t"] = np.ascontiguousarray(W["w1"].T)
    d["w2t"] = np.ascontiguousarray(W["w2"].T)
    d["cvw"] = np.ascontiguousarray(
        W["cw"].reshape(32, 128, 3).transpose(1, 0, 2))
    d["cvb"] = np.ascontiguousarray(W["cb"].reshape(32, 128).T)
    hs = np.zeros((128, 16), np.float32)
    for b in range(B):
        if core > 0:
            hs[(core - 1) * 16 + b * 2 + 1, b * 2 + 0] = 1.0
        if core < NCORE - 1:
            hs[(core + 1) * 16 + b * 2 + 0, b * 2 + 1] = 1.0
    d["hsel"] = hs
    return d


_NC_CACHE = {}


def kernel(**inputs):
    W = dict(
        Wq0=inputs["L0_attn_Wq"], Wk0=inputs["L0_attn_Wk"],
        Wv0=inputs["L0_attn_Wv"], rel=inputs["L0_attn_rel_table"],
        g0=inputs["L0_attn_gamma"], b0=inputs["L0_attn_beta"],
        Wc=inputs["L0_ffn_Wc"], bc=inputs["L0_ffn_bc"],
        W1=np.ascontiguousarray(
            inputs["L0_ffn_W1"].transpose(0, 1, 2, 3)
            .reshape(32, 4, D, FF).transpose(0, 1, 2, 3)
            .reshape(32 * 4, D, FF)
            .reshape(ES, D, FF)) if False else np.ascontiguousarray(
            inputs["L0_ffn_W1"].reshape(32, 4, D, FF)
            .transpose(0, 1, 2, 3).reshape(ES, D, FF)),
        b1=np.ascontiguousarray(inputs["L0_ffn_b1"].reshape(ES, FF)),
        W2=np.ascontiguousarray(inputs["L0_ffn_W2"].reshape(ES, D, FF)),
        b2=np.ascontiguousarray(inputs["L0_ffn_b2"].reshape(ES, D)),
        Wq1=inputs["L1_attn_Wq"], bq1=inputs["L1_attn_bq"],
        Wk1=inputs["L1_attn_Wk"], bk1=inputs["L1_attn_bk"],
        Wv1=inputs["L1_attn_Wv"], bv1=inputs["L1_attn_bv"],
        Wp1=inputs["L1_attn_Wp"], bp1=inputs["L1_attn_bp"],
        w1=inputs["L1_ffn_w1"], cw=inputs["L1_ffn_conv_w"],
        cb=inputs["L1_ffn_conv_b"], w2=inputs["L1_ffn_w2"],
    )
    W = {k: np.ascontiguousarray(np.asarray(v, dtype=np.float32))
         for k, v in W.items()}
    x = np.ascontiguousarray(np.asarray(inputs["L0_attn_x"], np.float32))
    in_maps = [_stage(c, x, W) for c in range(NCORE)]
    if "nc" not in _NC_CACHE:
        _NC_CACHE["nc"] = build()
    res = run_bass_kernel_spmd(_NC_CACHE["nc"], in_maps,
                               core_ids=list(range(NCORE)))
    out = np.empty((B, S, D), np.float32)
    for c in range(NCORE):
        o = res.results[c]["out_own"]
        out[:, c * SO:(c + 1) * SO, :] = (
            o.reshape(D, B, SO).transpose(1, 2, 0))
    return out


# version 06 (final)
# speedup vs baseline: 1.0212x; 1.0212x over previous
import numpy as np
import ml_dtypes
import concourse.bass as bass
import concourse.mybir as mybir
from concourse.bass_utils import run_bass_kernel_spmd

F32 = mybir.dt.float32
BF16 = mybir.dt.bfloat16
AX = mybir.AxisListType.X
ALU = mybir.AluOpType
ACT = mybir.ActivationFunctionType

NCORE = 8
B, S, D, H, HD = 8, 2048, 1024, 16, 64
SO = S // NCORE
T = B * SO
DC = D // 128
TC = T // 128
ES, FF, GRP = 128, 32, 32
DH = 4096
RNE = float(np.float32(3 * 2 ** 22))
BF = ml_dtypes.bfloat16


class Sem:
    def __init__(self, h):
        self.h = h
        self.n = 0


def build(debug=False):
    nc = bass.Bass("TRN2", target_bir_lowering=False, debug=False,
                   detect_race_conditions=False)

    def inp(name, shape, dt=F32):
        return nc.dram_tensor(name, shape, dt, kind="ExternalInput").ap()

    xt = inp("xt", [D, T])
    bt = inp("bt", [H, S, SO])
    wq0 = inp("wq0", [D, D]); wk0 = inp("wk0", [D, D]); wv0 = inp("wv0", [D, D])
    g0 = inp("g0", [128, DC]); b0 = inp("b0", [128, DC])
    wc = inp("wc", [D, ES]); bcb = inp("bcb", [128, 1])
    w1s = inp("w1s", [4, DC, 128, 32, FF])
    b1p = inp("b1p", [128, ES])
    w2s = inp("w2s", [4, 128, 16, DC, 128])
    b2p = inp("b2p", [ES, D])
    wq1 = inp("wq1", [D, D]); wk1 = inp("wk1", [D, D])
    wv1 = inp("wv1", [D, D]); wp1 = inp("wp1", [D, D])
    bq1 = inp("bq1", [128, DC]); bp1 = inp("bp1", [128, DC])
    bk1r = inp("bk1r", [1, D]); bv1r = inp("bv1r", [1, D])
    w1t = inp("w1t", [D, DH])
    w2t = inp("w2t", [DH, D])
    cvw = inp("cvw", [128, 32, 3])
    cvb = inp("cvb", [128, 32])
    hsel = inp("hsel", [128, 16])
    out_own = nc.dram_tensor("out_own", [D, T], F32, kind="ExternalOutput").ap()

    dbg = "ExternalOutput" if debug else "Internal"
    kv_in = nc.dram_tensor("kv_in", [D * T + T * H * 65], BF16)
    kv_all = nc.dram_tensor("kv_all", [NCORE, D * T + T * H * 65], BF16)
    att_d = nc.dram_tensor("att_d", [D, B, SO], F32, kind=dbg)
    r1t_d = nc.dram_tensor("r1t_d", [D, T], F32, kind=dbg)
    ffn_d = nc.dram_tensor("ffn_d", [D, T], F32, kind=dbg)
    r2t_d = nc.dram_tensor("r2t_d", [D, T], F32, kind=dbg)
    r3t_d = nc.dram_tensor("r3t_d", [D, T], F32, kind=dbg)
    kvs_in = nc.dram_tensor("kvs_in", [B * H * HD * HD], F32)
    kvs_out = nc.dram_tensor("kvs_out", [B * H * HD * HD], F32)
    halo_in = nc.dram_tensor("halo_in", [16, DH], BF16)
    halo_out = nc.dram_tensor("halo_out", [NCORE * 16, DH], BF16)
    w1q_d = nc.dram_tensor("w1q_d", [DC, 128, DH], BF16)
    w2q_d = nc.dram_tensor("w2q_d", [32, 128, D], BF16)
    at1_d = nc.dram_tensor("at1_d", [D, T], BF16)
    row_d = nc.dram_tensor("row_d", [4096], F32)
    bt_b = nc.dram_tensor("bt_b", [H * S * SO], BF16)
    wq0_b = nc.dram_tensor("wq0_b", [D * D], BF16)
    wk0_b = nc.dram_tensor("wk0_b", [D * D], BF16)
    wv0_b = nc.dram_tensor("wv0_b", [D * D], BF16)
    wq1_b = nc.dram_tensor("wq1_b", [D * D], BF16)
    wk1_b = nc.dram_tensor("wk1_b", [D * D], BF16)
    wv1_b = nc.dram_tensor("wv1_b", [D * D], BF16)
    wp1_b = nc.dram_tensor("wp1_b", [D * D], BF16)
    w1s_b = nc.dram_tensor("w1s_b", [4 * DC * 128 * 32 * FF], BF16)
    w2s_b = nc.dram_tensor("w2s_b", [4 * 128 * 16 * DC * 128], BF16)
    xt_b = nc.dram_tensor("xt_b", [D * T], BF16)
    dbg1 = nc.dram_tensor("dbg1", [128, 2048], F32, kind=dbg)
    dbg2 = nc.dram_tensor("dbg2", [128, 2048], F32, kind=dbg)
    dbg3 = nc.dram_tensor("dbg3", [128, 8192], BF16, kind=dbg)
    dbg4 = nc.dram_tensor("dbg4", [128, 512], BF16, kind=dbg)
    dbg5 = nc.dram_tensor("dbg5", [128, 32768], BF16, kind=dbg)
    kvi, kva = kv_in.ap(), kv_all.ap()

    persist = []
    def sbp(name, shape, dt):
        cm = nc.sbuf_tensor(name, shape, dt)
        t = cm.__enter__()
        persist.append(cm)
        return t

    identb = sbp("identb", [128, 128], BF16)
    identf = sbp("identf", [128, 128], F32)
    ones_f1 = sbp("ones_f1", [1, 128], F32)
    ones_ff = sbp("ones_ff", [128, 1], F32)
    tieb = sbp("tieb", [128, 2048], F32)
    rowa = sbp("rowa", [1, 2048], F32)
    rowb = sbp("rowb", [1, 2048], F32)
    misc = sbp("misc", [128, 512], F32)
    misc2 = sbp("misc2", [128, 512], F32)
    g0s = sbp("g0s", [128, DC], F32)
    b0s = sbp("b0s", [128, DC], F32)
    bcs = sbp("bcs", [128, 1], F32)
    b1s = sbp("b1s", [128, ES], F32)
    bq1s = sbp("bq1s", [128, DC], F32)
    bp1s = sbp("bp1s", [128, DC], F32)
    cvws = sbp("cvws", [128, 32, 3], F32)
    cvbs = sbp("cvbs", [128, 32], F32)
    hsels = sbp("hsels", [128, 16], BF16)

    from contextlib import ExitStack
    _top = ExitStack()
    _ec = _top.enter_context
    ps_a = _ec(nc.psum_tensor("ps_a", [128, 1024], F32))
    ps_b = _ec(nc.psum_tensor("ps_b", [128, 1024], F32))
    ps_s = _ec(nc.psum_tensor("ps_s", [128, 512], F32))
    ps_t = _ec(nc.psum_tensor("ps_t", [128, 2048], BF16))
    _sd = _ec(nc.semaphore("sd")); _sg = _ec(nc.semaphore("sg"))
    _st = _ec(nc.semaphore("st")); _sv = _ec(nc.semaphore("sv"))
    _ss = _ec(nc.semaphore("ss")); _sc = _ec(nc.semaphore("sc"))
    _so = _ec(nc.semaphore("so"))
    _sgd = _ec(nc.semaphore("sgd"))
    block = _ec(nc.Block())
    if True:
        sd, sg, st, sv, ss, sc, so = (Sem(_sd), Sem(_sg), Sem(_st), Sem(_sv),
                                      Sem(_ss), Sem(_sc), Sem(_so))
        sgd = Sem(_sgd)
        sems = [sd, sg, st, sv, ss, sc, so, sgd]
        S_, G_, T_, V_, C_ = nc.sync, nc.gpsimd, nc.tensor, nc.vector, nc.scalar
        ALLE = [S_, G_, T_, V_, C_]

        @block.sync
        def _(sync):
            def _wflush(e, xs):
                xs = list(xs)
                if sg in xs and sgd not in xs:
                    xs.append(sgd)
                pend = 0
                for i, x in enumerate(xs):
                    e.wait_ge(x.h, x.n)
                    pend += 1
                    if pend == 2 and i != len(xs) - 1:
                        e.sem_inc(xs[0].h, 0)
                        pend = 0

            def bar():
                for e in ALLE:
                    _wflush(e, sems)
                    e.sem_inc(sems[0].h, 0)

            def dS(out, in_, slow=False):
                S_.dma_start(out=out, in_=in_,
                             allow_slow_non_contiguous=slow
                             ).then_inc(sd.h, 16); sd.n += 16

            def dG(out, in_):
                G_.wait_ge(sg.h, sg.n)
                G_.wait_ge(sgd.h, sgd.n)
                G_.dma_start(out=out, in_=in_).then_inc(sgd.h, 16)
                sgd.n += 16

            def dO(out, in_):
                S_.dma_start(out=out, in_=in_).then_inc(so.h, 16); so.n += 16

            def tI(i):
                i.then_inc(st.h, 1); st.n += 1

            _lastT = {}

            def MM(*a, **k):
                i = T_.matmul(*a, **k)
                _lastT["i"] = i
                return i

            def TP(*a, **k):
                i = T_.transpose(*a, **k)
                _lastT["i"] = i
                return i

            def tmark():
                _lastT["i"].then_inc(st.h, 1); st.n += 1

            def vI(i):
                i.then_inc(sv.h, 1); sv.n += 1

            def cI(i):
                i.then_inc(ss.h, 1); ss.n += 1

            def wT(*xs):
                _wflush(T_, xs)

            def wV(*xs):
                _wflush(V_, xs)

            def wC(*xs):
                _wflush(C_, xs)

            def ln_fm(X, eps, sqt, sA, sB, gamma=None, beta=None):
                for half in range(2):
                    wT(sv, ss)
                    for dcx in range(DC):
                        for ntx in range(2):
                            c0 = half * 1024 + ntx * 512
                            MM(ps_a[0:1, ntx * 512:ntx * 512 + 512],
                                      ones_ff[:, :],
                                      X[:, dcx * 2048 + c0:dcx * 2048 + c0 + 512],
                                      start=(dcx == 0), stop=(dcx == DC - 1))
                    tmark()
                    wV(st)
                    vI(V_.tensor_copy(rowa[:, half * 1024:half * 1024 + 1024],
                                      ps_a[0:1, 0:1024]))
                for half in range(2):
                    wT(sv)
                    for dcx in range(DC):
                        c0 = half * 1024
                        if dcx == 0:
                            wC(sv)
                        cI(C_.activation(
                            sqt[:, 0:1024],
                            X[:, dcx * 2048 + c0:dcx * 2048 + c0 + 1024],
                            ACT.Square))
                        wT(ss)
                        for ntx in range(2):
                            MM(ps_b[0:1, ntx * 512:ntx * 512 + 512],
                                      ones_ff[:, :],
                                      sqt[:, ntx * 512:ntx * 512 + 512],
                                      start=(dcx == 0), stop=(dcx == DC - 1))
                        tmark()
                        wC(st)
                    wV(st)
                    vI(V_.tensor_copy(rowb[:, half * 1024:half * 1024 + 1024],
                                      ps_b[0:1, 0:1024]))
                vI(V_.tensor_scalar(out=rowa[:, :], in0=rowa[:, :],
                                    scalar1=1.0 / D, scalar2=None, op0=ALU.mult))
                vI(V_.tensor_scalar(out=rowb[:, :], in0=rowb[:, :],
                                    scalar1=1.0 / D, scalar2=None, op0=ALU.mult))
                vI(V_.tensor_tensor(sA[0:1, 0:2048], rowa[:, :], rowa[:, :],
                                    ALU.mult))
                vI(V_.tensor_tensor(rowb[:, :], rowb[:, :], sA[0:1, 0:2048],
                                    ALU.subtract))
                vI(V_.tensor_scalar(out=rowb[:, :], in0=rowb[:, :],
                                    scalar1=float(eps), scalar2=None, op0=ALU.add))
                wC(sv)
                cI(C_.activation(rowb[:, :], rowb[:, :], ACT.Sqrt))
                wV(ss)
                vI(V_.reciprocal(rowb[:, :], rowb[:, :]))
                vI(V_.tensor_tensor(rowa[:, :], rowa[:, :], rowb[:, :], ALU.mult))
                vI(V_.tensor_scalar(out=rowa[:, :], in0=rowa[:, :], scalar1=-1.0,
                                    scalar2=None, op0=ALU.mult))
                wT(sv)
                for half in range(2):
                    wT(ss)
                    for ntx in range(2):
                        c0 = half * 1024 + ntx * 512
                        MM(ps_a[:, ntx * 512:ntx * 512 + 512],
                                  ones_f1[:, :], rowb[0:1, c0:c0 + 512],
                                  start=True, stop=True)
                        MM(ps_b[:, ntx * 512:ntx * 512 + 512],
                                  ones_f1[:, :], rowa[0:1, c0:c0 + 512],
                                  start=True, stop=True)
                    tmark()
                    wC(st)
                    cI(C_.copy(sA[:, half * 1024:half * 1024 + 1024],
                               ps_a[:, 0:1024]))
                    cI(C_.copy(sB[:, half * 1024:half * 1024 + 1024],
                               ps_b[:, 0:1024]))
                wV(ss)
                for dcx in range(DC):
                    sl = X[:, dcx * 2048:(dcx + 1) * 2048]
                    vI(V_.tensor_tensor(sl, sl, sA[:, 0:2048], ALU.mult))
                    wV(sv)
                    vI(V_.tensor_tensor(sl, sl, sB[:, 0:2048], ALU.add))
                    if gamma is not None:
                        wC(sv)
                        cI(C_.activation(sl, sl, ACT.Identity,
                                         bias=beta[:, dcx:dcx + 1],
                                         scale=gamma[:, dcx:dcx + 1]))
                bar()

            G_.memset(identb[:, :], 0.0).then_inc(sg.h, 1); sg.n += 1
            G_.wait_ge(sg.h, sg.n)
            G_.affine_select(out=identb[:, :], in_=identb[:, :],
                             compare_op=ALU.not_equal, fill=1.0, base=0,
                             pattern=[[-1, 128]],
                             channel_multiplier=1).then_inc(sg.h, 1)
            sg.n += 1
            G_.wait_ge(sg.h, sg.n)
            G_.memset(identf[:, :], 0.0).then_inc(sg.h, 1); sg.n += 1
            G_.wait_ge(sg.h, sg.n)
            G_.affine_select(out=identf[:, :], in_=identf[:, :],
                             compare_op=ALU.not_equal, fill=1.0, base=0,
                             pattern=[[-1, 128]],
                             channel_multiplier=1).then_inc(sg.h, 1)
            sg.n += 1
            G_.wait_ge(sg.h, sg.n)
            G_.memset(ones_f1[:, :], 1.0).then_inc(sg.h, 1); sg.n += 1
            G_.wait_ge(sg.h, sg.n)
            G_.memset(ones_ff[:, :], 1.0).then_inc(sg.h, 1); sg.n += 1
            G_.wait_ge(sg.h, sg.n)
            G_.wait_ge(sgd.h, sgd.n)
            G_.iota(rowa[:, :], pattern=[[0, 64], [1, 32]], base=0,
                    channel_multiplier=0,
                    allow_small_or_imprecise_dtypes=True).then_inc(sg.h, 1)
            sg.n += 1
            wV(sg)
            vI(V_.tensor_scalar(out=rowa[:, :], in0=rowa[:, :],
                                scalar1=float(np.float32(1e-6) / np.float32(31.0)),
                                scalar2=None, op0=ALU.mult))
            dS(g0s[:, :], g0)
            dS(b0s[:, :], b0)
            dS(bcs[:, :], bcb)
            dS(b1s[:, :], b1p)
            dS(bq1s[:, :], bq1)
            dS(bp1s[:, :], bp1)
            dS(cvws[:, :, :], cvw)
            dS(cvbs[:, :], cvb)
            dG(hsels[:, :], hsel)
            wT(sv)
            for ntx in range(4):
                wT(ss)
                MM(ps_a[:, 0:512], ones_f1[:, :],
                          rowa[0:1, ntx * 512:(ntx + 1) * 512],
                          start=True, stop=True)
                tmark()
                wC(st)
                cI(C_.copy(tieb[:, ntx * 512:(ntx + 1) * 512], ps_a[:, 0:512]))
            bar()

            with (
                nc.sbuf_tensor("pf32", [128, 8192], F32) as pf32,
                nc.sbuf_tensor("pbf", [128, 8192], BF16) as pbf,
            ):
                jobs = []
                for srct, dstt in ((bt, bt_b), (wq0, wq0_b), (wk0, wk0_b),
                                   (wv0, wv0_b), (wq1, wq1_b), (wk1, wk1_b),
                                   (wv1, wv1_b), (wp1, wp1_b), (w1s, w1s_b),
                                   (w2s, w2s_b), (xt, xt_b)):
                    sf = 1
                    for dim in srct.shape:
                        sf *= dim
                    jobs.append((srct, dstt, sf))
                for srct, dstt, n_el in jobs:
                    flat_in = srct.rearrange(
                        " ".join("abcdefg"[:len(srct.shape)]) + " -> ("
                        + " ".join("abcdefg"[:len(srct.shape)]) + ")")
                    nt_ = n_el // (128 * 4096)
                    assert nt_ * 128 * 4096 == n_el, n_el
                    for k in range(nt_):
                        slot = (k % 2) * 4096
                        S_.wait_ge(sv.h, sv.n)
                        dS(pf32[:, slot:slot + 4096],
                           flat_in[k * 524288:(k + 1) * 524288]
                           .rearrange("(p q) -> p q", p=128))
                        wV(sd)
                        V_.wait_ge(sd.h, sd.n)
                        vI(V_.tensor_copy(pbf[:, slot:slot + 4096],
                                          pf32[:, slot:slot + 4096]))
                        S_.wait_ge(sv.h, sv.n)
                        dS(dstt.ap()[k * 524288:(k + 1) * 524288]
                           .rearrange("(p q) -> p q", p=128),
                           pbf[:, slot:slot + 4096])
                bar()

            _stk12 = ExitStack()
            xtb = _stk12.enter_context(nc.sbuf_tensor("xtb", [128, 16384], BF16))
            wt0 = _stk12.enter_context(nc.sbuf_tensor("wt0", [128, 8192], BF16))
            qt = _stk12.enter_context(nc.sbuf_tensor("qt", [128, 16384], BF16))
            ktv = _stk12.enter_context(nc.sbuf_tensor("ktv", [128, 16640], BF16))
            kto = _stk12.enter_context(nc.sbuf_tensor("kto", [128, 16384], BF16))
            if True:
                for dcx in range(DC):
                    dS(xtb[:, dcx * 2048:(dcx + 1) * 2048],
                       xt_b.ap()[dcx * 262144:(dcx + 1) * 262144]
                       .rearrange("(p q) -> p q", p=128))
                for dcx in range(DC):
                    dS(wt0[:, dcx * 1024:(dcx + 1) * 1024],
                       wq0_b.ap()[dcx * 131072:(dcx + 1) * 131072]
                       .rearrange("(p q) -> p q", p=128))
                bar()

                def proj_fm(dst, wsb, src):
                    ssb = ss.n
                    k = 0
                    for mcx in range(DC):
                        for ntx in range(2):
                            PS = ps_a if k % 2 == 0 else ps_b
                            if k >= 2:
                                T_.wait_ge(ss.h, ssb + k - 1)
                            for dcx in range(DC):
                                for nh in range(2):
                                    MM(
                                        PS[:, nh * 512:nh * 512 + 512],
                                        wsb[:, dcx * 1024 + mcx * 128:
                                            dcx * 1024 + (mcx + 1) * 128],
                                        src[:, dcx * 2048 + ntx * 1024 + nh * 512:
                                            dcx * 2048 + ntx * 1024 + nh * 512 + 512],
                                        start=(dcx == 0), stop=(dcx == DC - 1))
                            tmark()
                            wC(st)
                            cI(C_.copy(dst[:, mcx * 2048 + ntx * 1024:
                                           mcx * 2048 + ntx * 1024 + 1024],
                                       PS[:, 0:1024]))
                            k += 1

                proj_fm(qt, wt0, xtb)
                bar()
                for dcx in range(DC):
                    dS(wt0[:, dcx * 1024:(dcx + 1) * 1024],
                       wk0_b.ap()[dcx * 131072:(dcx + 1) * 131072]
                       .rearrange("(p q) -> p q", p=128))
                bar()
                proj_fm(kto, wt0, xtb)
                bar()
                for dcx in range(DC):
                    dS(wt0[:, dcx * 1024:(dcx + 1) * 1024],
                       wv0_b.ap()[dcx * 131072:(dcx + 1) * 131072]
                       .rearrange("(p q) -> p q", p=128))
                G_.wait_ge(sg.h, sg.n)
                G_.memset(ktv[:, :], 1.0).then_inc(sg.h, 1); sg.n += 1
                bar()
                ssb = ss.n
                for tcx in range(TC):
                    PS = ps_a if tcx % 2 == 0 else ps_b
                    if tcx >= 2:
                        T_.wait_ge(ss.h, ssb + tcx - 1)
                    for ntx in range(2):
                        for dcx in range(DC):
                            MM(PS[:, ntx * 512:(ntx + 1) * 512],
                                      xtb[:, dcx * 2048 + tcx * 128:
                                          dcx * 2048 + (tcx + 1) * 128],
                                      wt0[:, dcx * 1024 + ntx * 512:
                                          dcx * 1024 + (ntx + 1) * 512],
                                      start=(dcx == 0), stop=(dcx == DC - 1))
                    tmark()
                    wC(st)
                    cI(C_.copy(ktv[:, tcx * 1040:(tcx + 1) * 1040]
                               .rearrange("p (hh q) -> p hh q", q=65)[:, :, 0:64],
                               PS[:, 0:1024]
                               .rearrange("p (hh q) -> p hh q", q=64)))
                bar()
                for dcx in range(DC):
                    dS(kvi[dcx * 128 * T:(dcx + 1) * 128 * T]
                       .rearrange("(p t) -> p t", p=128),
                       kto[:, dcx * 2048:(dcx + 1) * 2048])
                for tcx in range(TC):
                    dS(kvi[D * T + tcx * 128 * 1040:
                           D * T + (tcx + 1) * 128 * 1040]
                       .rearrange("(p t) -> p t", p=128),
                       ktv[:, tcx * 1040:(tcx + 1) * 1040])
                bar()
                G_.wait_ge(sg.h, sg.n)
                G_.wait_ge(sgd.h, sgd.n)
                G_.collective_compute(
                    "AllGather", ALU.bypass, replica_groups=[list(range(NCORE))],
                    ins=[kvi], outs=[kva]).then_inc(sc.h, 1)
                sc.n += 1
                bar()

                KVT = D * T
                for b in range(B):
                    for src in range(NCORE):
                        for dcx in range(DC):
                            dS(xtb[:, dcx * 2048 + src * 256:
                                   dcx * 2048 + (src + 1) * 256],
                               kv_all[src, dcx * 128 * T:(dcx + 1) * 128 * T]
                               .rearrange("(p t) -> p t", p=128)
                               [:, b * 256:(b + 1) * 256])
                        for jcx in range(2):
                            kk = src * 2 + jcx
                            dS(ktv[:, kk * 1040:(kk + 1) * 1040],
                               kv_all[src, KVT + (b * 2 + jcx) * 128 * 1040:
                                      KVT + (b * 2 + jcx + 1) * 128 * 1040]
                               .rearrange("(p t) -> p t", p=128))
                    dS(wt0[:, 0:4096].rearrange("p (kc q) -> p kc q", kc=16),
                       bt_b.ap()[0:524288]
                       .rearrange("(kc p q) -> p kc q", p=128, q=256))
                    pf = {0: sd.n}
                    bar()
                    for h in range(H):
                        if h + 1 < H:
                            hb1 = ((h + 1) % 2) * 4096
                            S_.wait_ge(st.h, st.n - 1)
                            dS(wt0[:, hb1:hb1 + 4096]
                               .rearrange("p (kc q) -> p kc q", kc=16),
                               bt_b.ap()[(h + 1) * 524288:(h + 2) * 524288]
                               .rearrange("(kc p q) -> p kc q", p=128, q=256))
                            pf[h + 1] = sd.n
                        dcq, rq = h // 2, (h % 2) * 64
                        wT(sg, sv)
                        ss_base = ss.n
                        for kcx in range(16):
                            slt = (kcx % 2) * 512
                            if kcx >= 2:
                                T_.wait_ge(ss.h, ss_base + kcx - 1)
                            MM(ps_a[:, slt:slt + 256],
                                      xtb[rq:rq + 64, dcq * 2048 + kcx * 128:
                                          dcq * 2048 + (kcx + 1) * 128],
                                      qt[rq:rq + 64, dcq * 2048 + b * 256:
                                         dcq * 2048 + (b + 1) * 256],
                                      start=True, stop=True)
                            tmark()
                            wC(st)
                            cI(C_.activation(kto[:, kcx * 256:(kcx + 1) * 256],
                                             ps_a[:, slt:slt + 256], ACT.Exp,
                                             scale=1.0 / 32.0))
                        wT(ss, sv, sg)
                        for kcx in range(16):
                            MM(ps_b[0:65, 0:256],
                                      ktv[:, kcx * 1040 + h * 65:
                                          kcx * 1040 + h * 65 + 65],
                                      kto[:, kcx * 256:(kcx + 1) * 256],
                                      start=(kcx == 0), stop=(kcx == 15))
                        T_.wait_ge(sd.h, pf[h])
                        for kcx in range(16):
                            MM(ps_s[0:64, 0:256],
                                      ktv[:, kcx * 1040 + h * 65:
                                          kcx * 1040 + h * 65 + 64],
                                      wt0[:, (h % 2) * 4096 + kcx * 256:
                                          (h % 2) * 4096 + (kcx + 1) * 256],
                                      start=(kcx == 0), stop=(kcx == 15))
                        tmark()
                        wV(st)
                        V_.wait_ge(so.h, so.n)
                        vI(V_.reciprocal(rowa[0:1, 0:256], ps_b[64:65, 0:256]))
                        wT(sv)
                        MM(ps_a[0:64, 0:256], ones_f1[:, 0:64],
                                  rowa[0:1, 0:256], start=True, stop=True)
                        tmark()
                        wV(st)
                        vI(V_.tensor_copy(misc[0:64, 0:256], ps_a[0:64, 0:256]))
                        wV(sv)
                        vI(V_.tensor_tensor(misc2[0:64, 0:256], ps_b[0:64, 0:256],
                                            misc[0:64, 0:256], ALU.mult))
                        wV(sv)
                        vI(V_.tensor_tensor(misc2[0:64, 0:256],
                                            misc2[0:64, 0:256],
                                            ps_s[0:64, 0:256], ALU.add))
                        S_.wait_ge(sv.h, sv.n)
                        dO(att_d.ap()[h * 64:(h + 1) * 64, b, :],
                           misc2[0:64, 0:256])
                    bar()
            bar()

            _stk12.close()
            _stk4 = ExitStack()
            strm = _stk4.enter_context(nc.sbuf_tensor("strm", [128, 16384], F32))
            sqt = _stk4.enter_context(nc.sbuf_tensor("sqt", [128, 1024], F32))
            sA = _stk4.enter_context(nc.sbuf_tensor("sA", [128, 2048], F32))
            sB = _stk4.enter_context(nc.sbuf_tensor("sB", [128, 2048], F32))
            ev32 = _stk4.enter_context(nc.sbuf_tensor("ev32", [128, 2048], F32))
            if True:
                for dcx in range(DC):
                    for b in range(B):
                        dS(strm[:, dcx * 2048 + b * 256:
                                dcx * 2048 + (b + 1) * 256],
                           att_d.ap()[dcx * 128:(dcx + 1) * 128, b, :])
                bar()
                ln_fm(strm, 1e-3, sqt, sA, sB, gamma=g0s, beta=b0s)
                for dcx in range(DC):
                    S_.wait_ge(sv.h, sv.n)
                    dS(ev32[:, 0:2048], xt[dcx * 128:(dcx + 1) * 128, :])
                    wV(sd)
                    vI(V_.tensor_tensor(strm[:, dcx * 2048:(dcx + 1) * 2048],
                                        strm[:, dcx * 2048:(dcx + 1) * 2048],
                                        ev32[:, 0:2048], ALU.add))
                bar()
                ln_fm(strm, 1e-5, sqt, sA, sB)
                for dcx in range(DC):
                    dO(r1t_d.ap()[dcx * 128:(dcx + 1) * 128, :],
                       strm[:, dcx * 2048:(dcx + 1) * 2048])
                bar()
                _stk4.close()

                _stk5 = ExitStack()
                permb = _stk5.enter_context(nc.sbuf_tensor("permb", [128, 2048], BF16))
                permt = _stk5.enter_context(nc.sbuf_tensor("permt", [128, 2048], BF16))
                _stk5r = ExitStack()
                r1bk = _stk5r.enter_context(nc.sbuf_tensor("r1bk", [128, 16384], BF16))
                _stk5r2 = ExitStack()
                tmpc = _stk5r2.enter_context(nc.sbuf_tensor("tmpc", [128, 2048], BF16))
                if True:
                    for dcx in range(DC):
                        dG(tmpc[:, :], r1t_d.ap()[dcx * 128:(dcx + 1) * 128, :])
                        wT(sg, ss)
                        for tcx in range(TC):
                            slt = (tcx % 2) * 1024
                            wT(ss)
                            TP(ps_t[:, slt:slt + 128],
                               tmpc[:, tcx * 128:(tcx + 1) * 128],
                               identb[:, :])
                            tmark()
                            wC(st)
                            cI(C_.copy(r1bk[:, tcx * 1024 + dcx * 128:
                                            tcx * 1024 + (dcx + 1) * 128],
                                       ps_t[:, slt:slt + 128]))
                        bar()
                    _stk5r2.close()
                    bar()
                    _stk5l = ExitStack()
                    logit = _stk5l.enter_context(nc.sbuf_tensor("logit", [128, 2048], F32))
                    rt1 = _stk5l.enter_context(nc.sbuf_tensor("rt1", [128, 2048], F32))
                    rt2 = _stk5l.enter_context(nc.sbuf_tensor("rt2", [128, 2048], F32))
                    r1bs = _stk5l.enter_context(nc.sbuf_tensor("r1bs", [128, 512], BF16))
                    r1rs = _stk5l.enter_context(nc.sbuf_tensor("r1rs", [128, 512], BF16))
                    r1fs = _stk5l.enter_context(nc.sbuf_tensor("r1fs", [128, 512], F32))
                    wcb = _stk5l.enter_context(nc.sbuf_tensor("wcb", [128, 1024], BF16))
                    wcr = _stk5l.enter_context(nc.sbuf_tensor("wcr", [128, 1024], BF16))
                    fw = _stk5l.enter_context(nc.sbuf_tensor("fw", [128, 1024], F32))
                    if True:
                        for dcx in range(DC):
                            dS(fw[:, dcx * 128:(dcx + 1) * 128],
                               wc[dcx * 128:(dcx + 1) * 128, :])
                        bar()
                        vI(V_.tensor_copy(wcb[:, :], fw[:, :]))
                        wV(sv)
                        vI(V_.tensor_tensor(rt1[:, 0:1024], fw[:, :], wcb[:, :],
                                            ALU.subtract))
                        wV(sv)
                        vI(V_.tensor_copy(wcr[:, :], rt1[:, 0:1024]))
                        bar()
                        for ntx in range(4):
                            for dcx in range(DC):
                                wV(st)
                                V_.sem_inc(sd.h, 0)
                                V_.wait_ge(sd.h, sd.n)
                                G_.wait_ge(st.h, st.n)
                                S_.wait_ge(st.h, st.n)
                                S_.wait_ge(sv.h, sv.n)
                                dG(r1bs[:, :],
                                   r1t_d.ap()[dcx * 128:(dcx + 1) * 128,
                                              ntx * 512:(ntx + 1) * 512])
                                dS(r1fs[:, :],
                                   r1t_d.ap()[dcx * 128:(dcx + 1) * 128,
                                              ntx * 512:(ntx + 1) * 512])
                                V_.wait_ge(sgd.h, sgd.n)
                                V_.wait_ge(sd.h, sd.n)
                                V_.sem_inc(sd.h, 0)
                                vI(V_.tensor_tensor(r1rs[:, :], r1fs[:, :],
                                                    r1bs[:, :], ALU.subtract))
                                w_ = wcb[:, dcx * 128:(dcx + 1) * 128]
                                w2_ = wcr[:, dcx * 128:(dcx + 1) * 128]
                                wT(sv, sd, sg)
                                T_.wait_ge(ss.h, ss.n)
                                MM(ps_a[:, 0:512], w_, r1bs[:, :],
                                          start=(dcx == 0), stop=False)
                                MM(ps_a[:, 0:512], w_, r1rs[:, :],
                                          start=False, stop=False)
                                MM(ps_a[:, 0:512], w2_, r1bs[:, :],
                                          start=False, stop=(dcx == DC - 1))
                                tmark()
                            wC(st)
                            cI(C_.activation(logit[:, ntx * 512:(ntx + 1) * 512],
                                             ps_a[:, 0:512], ACT.Identity,
                                             bias=bcs[:, 0:1], scale=1.0))
                        bar()
                    wV(ss)
                    vI(V_.tensor_tensor(logit[:, :], logit[:, :], tieb[:, :],
                                        ALU.add))
                    wV(sv)
                    vI(V_.tensor_reduce(rt1[:, 0:64],
                                        logit[:, :].rearrange(
                                            "p (n t) -> p n t", t=32),
                                        AX, ALU.max))
                    wV(sv)
                    vI(V_.tensor_tensor(
                        rt2[:, 0:2048].rearrange("p (n t) -> p n t", t=32),
                        logit[:, :].rearrange("p (n t) -> p n t", t=32),
                        rt1[:, 0:64][:, :, None].broadcast_to([128, 64, 32]),
                        ALU.is_ge))
                    wV(sv)
                    vI(V_.tensor_copy(permb[:, :], rt2[:, 0:2048]))
                    if debug:
                        S_.wait_ge(sv.h, sv.n)
                        dO(dbg1.ap(), logit[:, :])
                        dO(dbg2.ap(), rt2[:, 0:2048])
                    bar()
                    _stk5l.close()
                    for tcx in range(TC):
                        sl8 = (tcx % 2) * 1024
                        wT(ss)
                        TP(ps_t[:, sl8:sl8 + 128],
                                     permb[:, tcx * 128:(tcx + 1) * 128],
                                     identb[:, :])
                        tmark()
                        wC(st)
                        cI(C_.copy(permt[:, tcx * 128:(tcx + 1) * 128],
                                   ps_t[:, sl8:sl8 + 128]))
                    bar()
                    _stk5b = ExitStack()
                    gth = _stk5b.enter_context(nc.sbuf_tensor("gth", [128, 8192], BF16))
                    rpt = _stk5b.enter_context(nc.sbuf_tensor("rpt", [128, 32768], BF16))
                    w1b = _stk5b.enter_context(nc.sbuf_tensor("w1b", [128, 8192], BF16))
                    w2b = _stk5b.enter_context(nc.sbuf_tensor("w2b", [128, 8192], BF16))
                    innr = _stk5b.enter_context(nc.sbuf_tensor("innr", [128, 512], BF16))
                    bdrh = _stk5b.enter_context(nc.sbuf_tensor("bdrh", [128, 256], BF16))
                    b2b = _stk5b.enter_context(nc.sbuf_tensor("b2b", [128, 1024], BF16))
                    rpg = _stk5b.enter_context(nc.sbuf_tensor("rpg", [128, 1024], BF16))
                    if True:
                        dG(b2b[:, :], b2p)
                        vI(V_.memset(innr[:, :], 0.0))
                        bar()
                        for half in range(2):
                            for j in range(4):
                                S_.wait_ge(st.h, st.n)
                                dS(w1b[:, :].rearrange(
                                       "p (a q) -> p a q", a=DC),
                                   w1s_b.ap()[j * 1048576:(j + 1) * 1048576]
                                   .rearrange("(a p q) -> p a q", a=DC, p=128))
                                G_.wait_ge(sg.h, sg.n)
                                G_.memset(bdrh[:, :], 0.0).then_inc(sg.h, 1)
                                sg.n += 1
                                bar()
                                for tcl in range(8):
                                    tcx = half * 8 + tcl
                                    eb = (tcl % 2) * 128
                                    wV(st)
                                    for gg in range(4):
                                        vI(V_.tensor_copy(
                                            bdrh[gg * 32:(gg + 1) * 32,
                                                 eb + gg * 32:eb + gg * 32 + 32],
                                            permt[gg * 32:(gg + 1) * 32,
                                                  tcx * 128 + j * 32:
                                                  tcx * 128 + j * 32 + 32]))
                                    wT(sv, ss)
                                    gsb = ss.n
                                    for dcx in range(DC):
                                        gslt = (dcx % 2) * 512
                                        if dcx >= 2:
                                            T_.wait_ge(ss.h, gsb + dcx - 1)
                                        MM(
                                            ps_a[:, gslt:gslt + 128],
                                            r1bk[:, tcx * 1024 + dcx * 128:
                                                 tcx * 1024 + (dcx + 1) * 128],
                                            bdrh[:, eb:eb + 128],
                                            start=True, stop=True)
                                        tmark()
                                        wC(st)
                                        cI(C_.copy(
                                            gth[:, dcx * 1024 + tcl * 128:
                                                dcx * 1024 + (tcl + 1) * 128],
                                            ps_a[:, gslt:gslt + 128]))
                                bar()
                                if debug and half == 0 and j == 0:
                                    dO(dbg3.ap(), gth[:, :])
                                    bar()
                                w1sb = ss.n
                                for e32 in range(32):
                                    es = j * 32 + e32
                                    rb = (e32 % 2) * 64
                                    PS = ps_a if e32 % 2 == 0 else ps_b
                                    if e32 >= 2:
                                        T_.wait_ge(ss.h, w1sb + e32 - 1)
                                    for dcx in range(DC):
                                        MM(
                                            PS[0:32, 0:32],
                                            w1b[:, dcx * 1024 + e32 * 32:
                                                dcx * 1024 + e32 * 32 + 32],
                                            gth[:, dcx * 1024:(dcx + 1) * 1024]
                                            .rearrange("p (tt e) -> p tt e",
                                                       e=32)[:, :, e32],
                                            start=(dcx == 0),
                                            stop=(dcx == DC - 1))
                                    tmark()
                                    wC(st)
                                    cI(C_.activation(
                                        innr[rb:rb + 32,
                                             (e32 // 2) * 32:
                                             (e32 // 2) * 32 + 32],
                                        PS[0:32, 0:32], ACT.Relu,
                                        bias=b1s[rb:rb + 32, es:es + 1],
                                        scale=1.0))
                                bar()
                                if debug and half == 0 and j == 0:
                                    dO(dbg4.ap(), innr[:, :])
                                    bar()
                                for pb_ in range(2):
                                    S_.wait_ge(st.h, st.n)
                                    dS(w2b[:, :].rearrange(
                                           "p (pr q) -> p pr q", pr=8),
                                       w2s_b.ap()[j * 2097152:(j + 1) * 2097152]
                                       .rearrange("(p pr q) -> p pr q",
                                                  p=128, q=1024)
                                       [:, pb_ * 8:(pb_ + 1) * 8, :])
                                    bar()
                                    w2sb = ss.n
                                    kk2 = 0
                                    for pr in range(8):
                                        for sub in range(2):
                                            e32 = (pb_ * 8 + pr) * 2 + sub
                                            es = j * 32 + e32
                                            rb = sub * 64
                                            for dcx in range(DC):
                                                PS = ps_a if kk2 % 2 == 0 else ps_b
                                                if kk2 >= 2:
                                                    T_.wait_ge(ss.h,
                                                               w2sb + kk2 - 1)
                                                MM(
                                                    PS[:, 0:32],
                                                    w2b[rb:rb + 32,
                                                        (pr * DC + dcx) * 128:
                                                        (pr * DC + dcx + 1) * 128],
                                                    innr[rb:rb + 32,
                                                         (e32 // 2) * 32:
                                                         (e32 // 2) * 32 + 32],
                                                    start=True, stop=True)
                                                tmark()
                                                wC(st)
                                                cI(C_.copy(
                                                    rpt[:, dcx * 4096:
                                                        (dcx + 1) * 4096]
                                                    .rearrange(
                                                        "p (n e) -> p n e",
                                                        e=128)[:, :, es],
                                                    PS[:, 0:32]))
                                                kk2 += 1
                                    bar()
                            if debug and half == 0:
                                dO(dbg5.ap(), rpt[:, :])
                                bar()
                            for gl in range(32):
                                gx = half * 32 + gl
                                tpb = ss.n
                                for dcx in range(DC):
                                    slt = (dcx % 2) * 1024
                                    if dcx >= 2:
                                        T_.wait_ge(ss.h, tpb + dcx - 1)
                                    TP(ps_t[:, slt:slt + 128],
                                       rpt[:, dcx * 4096 + gl * 128:
                                           dcx * 4096 + (gl + 1) * 128],
                                       identb[:, :])
                                    tmark()
                                    wC(st)
                                    cI(C_.copy(rpg[:, dcx * 128:(dcx + 1) * 128],
                                               ps_t[:, slt:slt + 128]))
                                wT(ss)
                                scb = sv.n
                                for dcx in range(DC):
                                    PS = ps_a if dcx % 2 == 0 else ps_b
                                    T_.wait_ge(sv.h, scb + dcx - 1)
                                    MM(PS[:, 0:32],
                                              rpg[:, dcx * 128:(dcx + 1) * 128],
                                              permb[:, gx * 32:(gx + 1) * 32],
                                              start=True, stop=False)
                                    MM(PS[:, 0:32],
                                              b2b[:, dcx * 128:(dcx + 1) * 128],
                                              permb[:, gx * 32:(gx + 1) * 32],
                                              start=False, stop=True)
                                    tmark()
                                    wV(st)
                                    V_.wait_ge(so.h, so.n - 16 * 7)
                                    vI(V_.tensor_copy(misc[:, (dcx % 8) * 32:
                                                           (dcx % 8) * 32 + 32],
                                                      PS[:, 0:32]))
                                    S_.wait_ge(sv.h, sv.n)
                                    dO(ffn_d.ap()[dcx * 128:(dcx + 1) * 128,
                                                  gx * 32:(gx + 1) * 32],
                                       misc[:, (dcx % 8) * 32:
                                            (dcx % 8) * 32 + 32])
                            bar()
                bar()

                _stk5b.close(); _stk5r.close(); _stk5.close()
                _stk6 = ExitStack()
                strm = _stk6.enter_context(nc.sbuf_tensor("strm2", [128, 16384], F32))
                _stkLN = ExitStack()
                sqt = _stkLN.enter_context(nc.sbuf_tensor("sqt2", [128, 1024], F32))
                sA = _stkLN.enter_context(nc.sbuf_tensor("sA2", [128, 2048], F32))
                sB = _stkLN.enter_context(nc.sbuf_tensor("sB2", [128, 2048], F32))
                ev32 = _stkLN.enter_context(nc.sbuf_tensor("ev322", [128, 2048], F32))
                for dcx in range(DC):
                    S_.wait_ge(sv.h, sv.n)
                    dS(strm[:, dcx * 2048:(dcx + 1) * 2048],
                       r1t_d.ap()[dcx * 128:(dcx + 1) * 128, :])
                    dS(ev32[:, 0:2048],
                       ffn_d.ap()[dcx * 128:(dcx + 1) * 128, :])
                    wV(sd)
                    vI(V_.tensor_tensor(strm[:, dcx * 2048:(dcx + 1) * 2048],
                                        strm[:, dcx * 2048:(dcx + 1) * 2048],
                                        ev32[:, 0:2048], ALU.add))
                bar()
                ln_fm(strm, 1e-5, sqt, sA, sB)
                if debug:
                    for dcx in range(DC):
                        dO(r2t_d.ap()[dcx * 128:(dcx + 1) * 128, :],
                           strm[:, dcx * 2048:(dcx + 1) * 2048])
                    bar()

                _stkLN.close()
                _stk7 = ExitStack()
                wt1 = _stk7.enter_context(nc.sbuf_tensor("wt1", [128, 8192], BF16))
                kpb = _stk7.enter_context(nc.sbuf_tensor("kpb", [128, 2048], BF16))
                vpb = _stk7.enter_context(nc.sbuf_tensor("vpb", [128, 2048], BF16))
                bkb = _stk7.enter_context(nc.sbuf_tensor("bkb", [128, 1024], F32))
                bvb = _stk7.enter_context(nc.sbuf_tensor("bvb", [128, 1024], F32))
                tq1 = _stk7.enter_context(nc.sbuf_tensor("tq1", [128, 1024], F32))
                tq2 = _stk7.enter_context(nc.sbuf_tensor("tq2", [128, 1024], F32))
                tq3 = _stk7.enter_context(nc.sbuf_tensor("tq3", [128, 1024], F32))
                _stk7r = ExitStack()
                r2b = _stk7r.enter_context(nc.sbuf_tensor("r2b", [128, 16384], BF16))
                if True:
                    def elu1(dst, src, tmp):
                        cI(C_.activation(tmp, src, ACT.Relu, scale=-1.0))
                        wC(ss)
                        cI(C_.activation(tmp, tmp, ACT.Exp, scale=-1.0))
                        vI(V_.tensor_scalar(out=tq3[:, 0:1024], in0=src,
                                            scalar1=0.0, scalar2=None,
                                            op0=ALU.max))
                        wV(ss, sv)
                        vI(V_.tensor_tensor(dst, tmp, tq3[:, 0:1024], ALU.add))

                    for dcx in range(DC):
                        vI(V_.tensor_copy(r2b[:, dcx * 2048:(dcx + 1) * 2048],
                                          strm[:, dcx * 2048:(dcx + 1) * 2048]))
                    dS(rowa[:, 0:1024], bk1r)
                    dS(rowb[:, 0:1024], bv1r)
                    bar()
                    for ntx in range(2):
                        MM(ps_a[:, ntx * 512:(ntx + 1) * 512],
                                  ones_f1[:, :],
                                  rowa[0:1, ntx * 512:(ntx + 1) * 512],
                                  start=True, stop=True)
                        MM(ps_b[:, ntx * 512:(ntx + 1) * 512],
                                  ones_f1[:, :],
                                  rowb[0:1, ntx * 512:(ntx + 1) * 512],
                                  start=True, stop=True)
                    tmark()
                    wC(st)
                    cI(C_.copy(bkb[:, :], ps_a[:, 0:1024]))
                    cI(C_.copy(bvb[:, :], ps_b[:, 0:1024]))
                    for b in range(B):
                        S_.wait_ge(st.h, st.n)
                        for dcx in range(DC):
                            dS(wt1[:, dcx * 1024:(dcx + 1) * 1024],
                               wk1_b.ap()[dcx * 131072:(dcx + 1) * 131072]
                               .rearrange("(p q) -> p q", p=128))
                        bar()
                        for i2 in range(2):
                            tcx = b * 2 + i2
                            PS = ps_a if i2 % 2 == 0 else ps_b
                            wT(ss, sv) if i2 == 0 else None
                            for dcx in range(DC):
                                for nh in range(2):
                                    MM(
                                        PS[:, nh * 512:nh * 512 + 512],
                                        r2b[:, dcx * 2048 + tcx * 128:
                                            dcx * 2048 + (tcx + 1) * 128],
                                        wt1[:, dcx * 1024 + nh * 512:
                                            dcx * 1024 + nh * 512 + 512],
                                        start=(dcx == 0),
                                        stop=(dcx == DC - 1))
                            tmark()
                            wV(st, ss)
                            vI(V_.tensor_tensor(tq1[:, :], PS[:, 0:1024],
                                                bkb[:, :], ALU.add))
                            wC(sv)
                            elu1(kpb[:, i2 * 1024:(i2 + 1) * 1024], tq1[:, :],
                                 tq2[:, :])
                        S_.wait_ge(st.h, st.n)
                        for dcx in range(DC):
                            dS(wt1[:, dcx * 1024:(dcx + 1) * 1024],
                               wv1_b.ap()[dcx * 131072:(dcx + 1) * 131072]
                               .rearrange("(p q) -> p q", p=128))
                        bar()
                        for i2 in range(2):
                            tcx = b * 2 + i2
                            wT(ss, sv)
                            for dcx in range(DC):
                                for nh in range(2):
                                    MM(
                                        ps_a[:, nh * 512:nh * 512 + 512],
                                        r2b[:, dcx * 2048 + tcx * 128:
                                            dcx * 2048 + (tcx + 1) * 128],
                                        wt1[:, dcx * 1024 + nh * 512:
                                            dcx * 1024 + nh * 512 + 512],
                                        start=(dcx == 0),
                                        stop=(dcx == DC - 1))
                            tmark()
                            wV(st)
                            vI(V_.tensor_tensor(vpb[:, i2 * 1024:
                                                    (i2 + 1) * 1024],
                                                ps_a[:, 0:1024], bvb[:, :],
                                                ALU.add))
                        V_.wait_ge(sd.h, sd.n)
                        wT(sv, ss)
                        for h in range(H):
                            slt = (h % 2) * 512
                            wT(sv)
                            for i2 in range(2):
                                MM(ps_b[0:64, slt:slt + 64],
                                          kpb[:, i2 * 1024 + h * 64:
                                              i2 * 1024 + h * 64 + 64],
                                          vpb[:, i2 * 1024 + h * 64:
                                              i2 * 1024 + h * 64 + 64],
                                          start=(i2 == 0), stop=(i2 == 1))
                            tmark()
                            wV(st)
                            vI(V_.tensor_copy(tq2[0:64, h * 64:(h + 1) * 64],
                                              ps_b[0:64, slt:slt + 64]))
                        S_.wait_ge(sv.h, sv.n)
                        dS(kvs_in.ap()[b * H * HD * HD:(b + 1) * H * HD * HD]
                           .rearrange("(h p q) -> p h q", p=64, q=64),
                           tq2[0:64, 0:1024]
                           .rearrange("p (h q) -> p h q", q=64))
                    bar()
                    G_.wait_ge(sg.h, sg.n)
                    G_.wait_ge(sgd.h, sgd.n)
                    G_.collective_compute(
                        "AllReduce", ALU.add,
                        replica_groups=[list(range(NCORE))],
                        ins=[kvs_in.ap()], outs=[kvs_out.ap()]).then_inc(sc.h, 1)
                    sc.n += 1
                    bar()
                    for dcx in range(DC):
                        dS(wt1[:, dcx * 1024:(dcx + 1) * 1024],
                           wq1_b.ap()[dcx * 131072:(dcx + 1) * 131072]
                           .rearrange("(p q) -> p q", p=128))
                    bar()
                    _stk7q = ExitStack()
                    qp = _stk7q.enter_context(nc.sbuf_tensor("qp", [128, 16384], BF16))
                    qpb = ss.n
                    kq = 0
                    for mcx in range(DC):
                        for ntx in range(2):
                            PS = ps_a if kq % 2 == 0 else ps_b
                            if kq >= 2:
                                T_.wait_ge(ss.h, qpb + 3 * (kq - 1))
                            for dcx in range(DC):
                                for nh in range(2):
                                    MM(
                                        PS[:, nh * 512:nh * 512 + 512],
                                        wt1[:, dcx * 1024 + mcx * 128:
                                            dcx * 1024 + (mcx + 1) * 128],
                                        r2b[:, dcx * 2048 + ntx * 1024 + nh * 512:
                                            dcx * 2048 + ntx * 1024 + nh * 512 + 512],
                                        start=(dcx == 0),
                                        stop=(dcx == DC - 1))
                            tmark()
                            wC(st, sv)
                            cI(C_.activation(tq1[:, :], PS[:, 0:1024],
                                             ACT.Identity,
                                             bias=bq1s[:, mcx:mcx + 1],
                                             scale=1.0))
                            wV(ss)
                            elu1(qp[:, mcx * 2048 + ntx * 1024:
                                    mcx * 2048 + ntx * 1024 + 1024],
                                 tq1[:, :], tq2[:, :])
                            kq += 1
                    bar()
                    dG(wt1[0:64, 0:8192].rearrange(
                           "p (b h q) -> p b h q", b=B, h=H),
                       kvs_out.ap().rearrange("(b h p q) -> p b h q",
                                              p=HD, q=HD, h=H))
                    dG(wt1[64:128, 0:8192].rearrange(
                           "p (b h q) -> p b h q", b=B, h=H),
                       kvs_out.ap().rearrange("(b h p q) -> p b h q",
                                              p=HD, q=HD, h=H))
                    bar()
                    for b in range(B):
                        for h in range(H):
                            rq, mq = (h % 2) * 64, h // 2
                            C_.wait_ge(so.h, so.n)
                            wC(st)
                            wT(ss)
                            MM(ps_s[0:64, 0:256],
                                      wt1[rq:rq + 64, (b * H + h) * 64:
                                          (b * H + h + 1) * 64],
                                      qp[rq:rq + 64, mq * 2048 + b * 256:
                                         mq * 2048 + (b + 1) * 256],
                                      start=True, stop=True)
                            tmark()
                            wC(st)
                            cI(C_.copy(kpb[0:64, 0:256], ps_s[0:64, 0:256]))
                            S_.wait_ge(ss.h, ss.n)
                            dO(at1_d.ap()[mq * 256 + rq:mq * 256 + rq + 64,
                                          b * 256:(b + 1) * 256]
                               if False else
                               at1_d.ap()[(h // 2) * 128 + (h % 2) * 64:
                                          (h // 2) * 128 + (h % 2) * 64 + 64,
                                          b * 256:(b + 1) * 256],
                               kpb[0:64, 0:256])
                    bar()
                    _stk7q.close()
                    _stk7r.close()
                    for dcx in range(DC):
                        dS(wt1[:, dcx * 1024:(dcx + 1) * 1024],
                           wp1_b.ap()[dcx * 131072:(dcx + 1) * 131072]
                           .rearrange("(p q) -> p q", p=128))
                    bar()
                    for mcx in range(DC):
                        for ntx in range(2):
                            wT(ss, sv)
                            for dcx in range(DC):
                                hb = (dcx % 2) * 1024
                                G_.wait_ge(st.h, st.n)
                                dG(kpb[:, hb:hb + 1024],
                                   at1_d.ap()[dcx * 128:(dcx + 1) * 128,
                                              ntx * 1024:(ntx + 1) * 1024])
                                wT(sg)
                                for nh in range(2):
                                    MM(
                                        ps_a[:, nh * 512:nh * 512 + 512],
                                        wt1[:, dcx * 1024 + mcx * 128:
                                            dcx * 1024 + (mcx + 1) * 128],
                                        kpb[:, hb + nh * 512:
                                            hb + nh * 512 + 512],
                                        start=(dcx == 0),
                                        stop=(dcx == DC - 1))
                                tmark()
                            wC(st)
                            cI(C_.activation(tq1[:, :], ps_a[:, 0:1024],
                                             ACT.Identity,
                                             bias=bp1s[:, mcx:mcx + 1],
                                             scale=1.0))
                            wV(ss)
                            vI(V_.tensor_tensor(
                                strm[:, mcx * 2048 + ntx * 1024:
                                     mcx * 2048 + ntx * 1024 + 1024],
                                strm[:, mcx * 2048 + ntx * 1024:
                                     mcx * 2048 + ntx * 1024 + 1024],
                                tq1[:, :], ALU.add))
                    bar()
                _stk7.close()
                _stkLN = ExitStack()
                sqt = _stkLN.enter_context(nc.sbuf_tensor("sqt3", [128, 1024], F32))
                sA = _stkLN.enter_context(nc.sbuf_tensor("sA3", [128, 2048], F32))
                sB = _stkLN.enter_context(nc.sbuf_tensor("sB3", [128, 2048], F32))
                ev32 = _stkLN.enter_context(nc.sbuf_tensor("ev323", [128, 2048], F32))
                ln_fm(strm, 1e-5, sqt, sA, sB)
                if debug:
                    for dcx in range(DC):
                        dO(r3t_d.ap()[dcx * 128:(dcx + 1) * 128, :],
                           strm[:, dcx * 2048:(dcx + 1) * 2048])
                    bar()

                _stkLN.close()
                _stk8 = ExitStack()
                xq1 = _stk8.enter_context(nc.sbuf_tensor("xq1", [128, 16384], BF16))
                wqb = _stk8.enter_context(nc.sbuf_tensor("wqb", [128, 8192], BF16))
                accm = _stk8.enter_context(nc.sbuf_tensor("accm", [128, 2048], F32))
                accn = _stk8.enter_context(nc.sbuf_tensor("accn", [128, 2048], F32))
                s1b = _stk8.enter_context(nc.sbuf_tensor("s1b", [128, 2048], F32))
                rs1b = _stk8.enter_context(nc.sbuf_tensor("rs1b", [128, 2048], F32))
                halT = _stk8.enter_context(nc.sbuf_tensor("halT", [128, 512], BF16))
                sw2s = _stk8.enter_context(nc.sbuf_tensor("sw2s", [128, 2], F32))
                _stk8b = ExitStack()
                wtf = _stk8b.enter_context(nc.sbuf_tensor("wtf", [128, 4096], F32))
                heg = _stk8b.enter_context(nc.sbuf_tensor("heg", [16, 4096], BF16))
                xq1e = _stk8b.enter_context(nc.sbuf_tensor("xq1e", [128, 128], BF16))
                if True:
                    def token_stats(src, cols, nblk):
                        wV(ss, st)
                        vI(V_.memset(misc[:, 0:256], 0.0))
                        vI(V_.memset(misc2[:, 0:256], 0.0))
                        """per-token max over partitions of src[:, 0:cols];
                        rowa <- 127/max(m,1e-5); rowb <- max(m,1e-5)/127
                        (flattened to [1, cols])."""
                        for i in range(nblk):
                            T_.wait_ge(sv.h, sv.n)
                            TP(ps_a[:, 0:128],
                                         src[:, i * 128:(i + 1) * 128],
                                         identf[:, :])
                            tmark()
                            wV(st)
                            vI(V_.tensor_reduce(misc[:, i:i + 1],
                                                ps_a[:, 0:128], AX, ALU.max))
                        wV(sv)
                        vI(V_.tensor_scalar(out=misc[:, 0:nblk],
                                            in0=misc[:, 0:nblk], scalar1=1e-5,
                                            scalar2=None, op0=ALU.max))
                        wV(sv)
                        vI(V_.tensor_scalar(out=misc2[:, 0:nblk],
                                            in0=misc[:, 0:nblk],
                                            scalar1=1.0 / 127.0, scalar2=None,
                                            op0=ALU.mult))
                        vI(V_.reciprocal(misc[:, 0:nblk], misc[:, 0:nblk]))
                        wV(sv)
                        vI(V_.tensor_scalar(out=misc[:, 0:nblk],
                                            in0=misc[:, 0:nblk], scalar1=127.0,
                                            scalar2=None, op0=ALU.mult))
                        wT(sv)
                        TP(ps_a[:, 0:128], misc[:, 0:128], identf[:, :])
                        TP(ps_b[:, 0:128], misc2[:, 0:128],
                                     identf[:, :])
                        tmark()
                        wC(st)
                        cI(C_.copy(misc[:, 128:256], ps_a[:, 0:128]))
                        cI(C_.copy(misc2[:, 128:256], ps_b[:, 0:128]))
                        S_.wait_ge(ss.h, ss.n)
                        dS(row_d.ap()[0:cols].rearrange("(a q) -> a q", a=nblk),
                           misc[0:nblk, 128:256])
                        dS(row_d.ap()[2048:2048 + cols]
                           .rearrange("(a q) -> a q", a=nblk),
                           misc2[0:nblk, 128:256])
                        bar()
                        dS(rowa[:, 0:cols],
                           row_d.ap()[0:cols].rearrange("(o q) -> o q", o=1))
                        dS(rowb[:, 0:cols],
                           row_d.ap()[2048:2048 + cols]
                           .rearrange("(o q) -> o q", o=1))
                        bar()

                    def bcast2(dstA, dstB, ncols):
                        wT(sv, ss)
                        for blk in range(ncols // 512):
                            c0 = blk * 512
                            wT(ss)
                            MM(ps_a[:, 0:512], ones_f1[:, :],
                               rowa[0:1, c0:c0 + 512], start=True, stop=True)
                            MM(ps_b[:, 0:512], ones_f1[:, :],
                               rowb[0:1, c0:c0 + 512], start=True, stop=True)
                            tmark()
                            wC(st)
                            cI(C_.copy(dstA[:, c0:c0 + 512], ps_a[:, 0:512]))
                            cI(C_.copy(dstB[:, c0:c0 + 512], ps_b[:, 0:512]))
                        bar()

                    vI(V_.tensor_copy(accm[:, :], strm[:, 0:2048]))
                    vI(V_.tensor_copy(accn[:, :], strm[:, 0:2048]))
                    wV(sv)
                    for dcx in range(1, DC):
                        vI(V_.tensor_tensor(accm[:, :], accm[:, :],
                                            strm[:, dcx * 2048:(dcx + 1) * 2048],
                                            ALU.max))
                        vI(V_.tensor_tensor(accn[:, :], accn[:, :],
                                            strm[:, dcx * 2048:(dcx + 1) * 2048],
                                            ALU.min))
                    wV(sv)
                    vI(V_.tensor_scalar(out=accn[:, :], in0=accn[:, :],
                                        scalar1=-1.0, scalar2=None, op0=ALU.mult))
                    wV(sv)
                    vI(V_.tensor_tensor(accm[:, :], accm[:, :], accn[:, :],
                                        ALU.max))
                    bar()
                    token_stats(accm, 2048, 16)
                    bcast2(s1b, rs1b, 2048)
                    for dcx in range(DC):
                        wV(ss)
                        vI(V_.tensor_tensor(accm[:, :],
                                            strm[:, dcx * 2048:(dcx + 1) * 2048],
                                            s1b[:, :], ALU.mult))
                        wV(sv)
                        vI(V_.tensor_scalar(
                            out=xq1[:, dcx * 2048:(dcx + 1) * 2048],
                            in0=accm[:, :], scalar1=RNE, scalar2=RNE,
                            op0=ALU.add, op1=ALU.subtract))
                    if debug:
                        S_.wait_ge(sv.h, sv.n)
                        S_.wait_ge(ss.h, ss.n)
                        dO(dbg1.ap(), s1b[:, :])
                        dO(dbg5.ap()[0:128, 0:16384], xq1[:, :])
                    bar()
                    vI(V_.memset(misc[:, 0:1], 0.0))
                    for dcx in range(DC):
                        dS(wtf[:, :], w1t[dcx * 128:(dcx + 1) * 128, :])
                        wV(sd, sv)
                        vI(V_.tensor_reduce(misc2[:, 0:1], wtf[:, :], AX,
                                            ALU.add, apply_absolute_value=True))
                        wV(sv)
                        vI(V_.tensor_tensor(misc[:, 0:1], misc[:, 0:1],
                                            misc2[:, 0:1], ALU.add))
                    wT(sv)
                    MM(ps_s[0:1, 0:1], ones_ff[:, :], misc[:, 0:1],
                              start=True, stop=True)
                    tmark()
                    wV(st)
                    vI(V_.tensor_scalar(out=rowa[0:1, 0:1], in0=ps_s[0:1, 0:1],
                                        scalar1=1.0 / (D * DH), scalar2=1e-5,
                                        op0=ALU.mult, op1=ALU.max))
                    wV(sv)
                    vI(V_.reciprocal(rowa[0:1, 1:2], rowa[0:1, 0:1]))
                    wT(sv)
                    MM(ps_s[:, 0:2], ones_f1[:, :], rowa[0:1, 0:2],
                              start=True, stop=True)
                    tmark()
                    wC(st)
                    cI(C_.copy(misc[:, 0:2], ps_s[:, 0:2]))
                    bar()
                    for dcx in range(DC):
                        dS(wtf[:, :], w1t[dcx * 128:(dcx + 1) * 128, :])
                        wV(sd, ss)
                        vI(V_.tensor_scalar(out=wtf[:, :], in0=wtf[:, :],
                                            scalar1=misc[:, 1:2], scalar2=RNE,
                                            op0=ALU.mult, op1=ALU.add))
                        wV(sv)
                        vI(V_.tensor_scalar(out=wtf[:, :], in0=wtf[:, :],
                                            scalar1=RNE, scalar2=1.0,
                                            op0=ALU.subtract, op1=ALU.min))
                        wV(sv)
                        vI(V_.tensor_scalar(out=wqb[:, 0:4096], in0=wtf[:, :],
                                            scalar1=-1.0, scalar2=None,
                                            op0=ALU.max))
                        S_.wait_ge(sv.h, sv.n)
                        dS(w1q_d.ap()[dcx], wqb[:, 0:4096])
                    for half in range(2):
                        wV(ss, sd)
                        vI(V_.tensor_scalar(
                            out=rs1b[:, half * 1024:half * 1024 + 1024],
                            in0=rs1b[:, half * 1024:half * 1024 + 1024],
                            scalar1=misc[:, 0:1], scalar2=None, op0=ALU.mult))
                    bar()
                    wV(sd)
                    for dcx in range(DC):
                        for e2, off in ((0, 0), (1, 255)):
                            vI(V_.tensor_copy(
                                xq1e[:, dcx * 16:(dcx + 1) * 16]
                                .rearrange("p (bb e) -> p bb e", e=2)[:, :, e2],
                                xq1[:, dcx * 2048:(dcx + 1) * 2048]
                                .rearrange("p (bb q) -> p bb q", q=256)
                                [:, :, off]))
                    dS(misc2[0:16:2, 2:3],
                       row_d.ap()[2048:4096:256][:, None], slow=True)
                    dS(misc2[1:16:2, 2:3],
                       row_d.ap()[2048 + 255:4096:256][:, None], slow=True)
                    bar()
                    vI(V_.tensor_scalar(out=misc2[0:16, 2:3],
                                        in0=misc2[0:16, 2:3],
                                        scalar1=misc[0:16, 0:1], scalar2=None,
                                        op0=ALU.mult))
                    bar()
                    for nb in range(8):
                        wC(sv)
                        for dcx in range(DC):
                            S_.wait_ge(st.h, st.n)
                            dS(wqb[:, 0:512],
                               w1q_d.ap()[dcx, :, nb * 512:(nb + 1) * 512])
                            wT(sd)
                            wT(ss)
                            MM(ps_s[0:16, 0:512],
                                      xq1e[:, dcx * 16:(dcx + 1) * 16],
                                      wqb[:, 0:512],
                                      start=(dcx == 0), stop=(dcx == DC - 1))
                            tmark()
                        wC(st)
                        cI(C_.activation(heg[0:16, nb * 512:(nb + 1) * 512],
                                         ps_s[0:16, 0:512], ACT.Identity,
                                         bias=0.0, scale=misc2[0:16, 2:3]))
                    bar()
                    dS(halo_in.ap(), heg[0:16, :])
                    bar()
                    G_.wait_ge(sg.h, sg.n)
                    G_.wait_ge(sgd.h, sgd.n)
                    G_.collective_compute(
                        "AllGather", ALU.bypass,
                        replica_groups=[list(range(NCORE))],
                        ins=[halo_in.ap().rearrange("a b -> (a b)")],
                        outs=[halo_out.ap().rearrange("a b -> (a b)")]
                    ).then_inc(sc.h, 1)
                    sc.n += 1
                    bar()
                    dS(wqb[:, 0:4096], halo_out.ap())
                    bar()
                    for nb in range(8):
                        wT(ss)
                        MM(ps_s[0:16, 0:512], hsels[:, :],
                                  wqb[:, nb * 512:(nb + 1) * 512],
                                  start=True, stop=True)
                        tmark()
                        wC(st)
                        cI(C_.copy(heg[0:16, nb * 512:(nb + 1) * 512],
                                   ps_s[0:16, 0:512]))
                    bar()
                    for hcx in range(32):
                        slt = (hcx % 2) * 1024
                        T_.wait_ge(ss.h, ss.n)
                        TP(ps_t[:, slt:slt + 16],
                                     heg[0:16, hcx * 128:(hcx + 1) * 128],
                                     identb[0:16, 0:16])
                        tmark()
                        wC(st)
                        cI(C_.copy(halT[:, hcx * 16:(hcx + 1) * 16],
                                   ps_t[:, slt:slt + 16]))
                    bar()
                    vI(V_.memset(misc[:, 0:1], 0.0))
                    for hcx in range(32):
                        dS(wtf[:, 0:1024], w2t[hcx * 128:(hcx + 1) * 128, :])
                        wV(sd, sv)
                        vI(V_.tensor_reduce(misc2[:, 0:1], wtf[:, 0:1024], AX,
                                            ALU.add, apply_absolute_value=True))
                        wV(sv)
                        vI(V_.tensor_tensor(misc[:, 0:1], misc[:, 0:1],
                                            misc2[:, 0:1], ALU.add))
                    wT(sv)
                    MM(ps_s[0:1, 0:1], ones_ff[:, :], misc[:, 0:1],
                              start=True, stop=True)
                    tmark()
                    wV(st)
                    vI(V_.tensor_scalar(out=rowa[0:1, 0:1], in0=ps_s[0:1, 0:1],
                                        scalar1=1.0 / (D * DH), scalar2=1e-5,
                                        op0=ALU.mult, op1=ALU.max))
                    wV(sv)
                    vI(V_.reciprocal(rowa[0:1, 1:2], rowa[0:1, 0:1]))
                    wT(sv)
                    MM(ps_s[:, 0:2], ones_f1[:, :], rowa[0:1, 0:2],
                              start=True, stop=True)
                    tmark()
                    wC(st)
                    cI(C_.copy(misc[:, 0:2], ps_s[:, 0:2]))
                    bar()
                    for hcx in range(32):
                        dS(wtf[:, 0:1024], w2t[hcx * 128:(hcx + 1) * 128, :])
                        wV(sd, ss)
                        vI(V_.tensor_scalar(out=wtf[:, 0:1024],
                                            in0=wtf[:, 0:1024],
                                            scalar1=misc[:, 1:2], scalar2=RNE,
                                            op0=ALU.mult, op1=ALU.add))
                        wV(sv)
                        vI(V_.tensor_scalar(out=wtf[:, 0:1024],
                                            in0=wtf[:, 0:1024],
                                            scalar1=RNE, scalar2=1.0,
                                            op0=ALU.subtract, op1=ALU.min))
                        wV(sv)
                        vI(V_.tensor_scalar(out=wqb[:, 0:1024],
                                            in0=wtf[:, 0:1024],
                                            scalar1=-1.0, scalar2=None,
                                            op0=ALU.max))
                        S_.wait_ge(sv.h, sv.n)
                        dS(w2q_d.ap()[hcx], wqb[:, 0:1024])
                    bar()
                    vI(V_.tensor_copy(sw2s[:, 0:2], misc[:, 0:2]))
                    bar()

                    _stk8b.close()
                    _stk8c = ExitStack()
                    hq = _stk8c.enter_context(nc.sbuf_tensor("hq", [128, 16384], BF16))
                    for tq in range(4):
                        tc0 = tq * 512
                        hvb = sv.n
                        dS(wqb[:, 0:1024]
                           .rearrange("p (a q) -> p a q", a=DC),
                           w1q_d.ap()[:, :, 0:128]
                           .rearrange("a p q -> p a q"))
                        pfh = {0: sd.n}
                        for hcx in range(32):
                            PS = ps_a if hcx % 2 == 0 else ps_b
                            if hcx + 1 < 32:
                                S_.wait_ge(st.h, st.n)
                                dS(wqb[:, ((hcx + 1) % 2) * 1024:
                                       ((hcx + 1) % 2) * 1024 + 1024]
                                   .rearrange("p (a q) -> p a q", a=DC),
                                   w1q_d.ap()[:, :, (hcx + 1) * 128:
                                              (hcx + 2) * 128]
                                   .rearrange("a p q -> p a q"))
                                pfh[hcx + 1] = sd.n
                            T_.wait_ge(sd.h, pfh[hcx])
                            if hcx >= 2:
                                T_.wait_ge(sv.h, hvb + hcx - 1)
                            for dcx in range(DC):
                                MM(PS[:, 0:512],
                                          wqb[:, (hcx % 2) * 1024 + dcx * 128:
                                              (hcx % 2) * 1024 + (dcx + 1) * 128],
                                          xq1[:, dcx * 2048 + tc0:
                                              dcx * 2048 + tc0 + 512],
                                          start=(dcx == 0),
                                          stop=(dcx == DC - 1))
                            tmark()
                            wV(st)
                            vI(V_.tensor_tensor(
                                hq[:, hcx * 512:(hcx + 1) * 512],
                                PS[:, 0:512], rs1b[:, tc0:tc0 + 512],
                                ALU.mult))
                        if debug and tq == 0:
                            S_.wait_ge(sv.h, sv.n)
                            S_.wait_ge(ss.h, ss.n)
                            dO(dbg3.ap(), hq[:, 0:8192])
                            dO(dbg2.ap(), rs1b[:, :])
                            bar()
                        bar()
                        hv = hq[:, 0:16384].rearrange("p (hh q) -> p hh q",
                                                      q=512)
                        w0b = cvws[:, :, 0][:, :, None]
                        w1b_ = cvws[:, :, 1][:, :, None]
                        w2b_ = cvws[:, :, 2][:, :, None]
                        for hq4 in range(8):
                            h0 = hq4 * 4
                            src = hv[:, h0:h0 + 4, :]
                            a4 = accm[:, 0:2048].rearrange(
                                "p (hh q) -> p hh q", q=512)
                            t4 = accn[:, 0:2048].rearrange(
                                "p (hh q) -> p hh q", q=512)
                            wV(ss, sv)
                            vI(V_.tensor_tensor(
                                a4, src, w1b_[:, h0:h0 + 4, :]
                                .broadcast_to([128, 4, 512]), ALU.mult))
                            vI(V_.tensor_tensor(
                                t4[:, :, 1:512], src[:, :, 0:511],
                                w0b[:, h0:h0 + 4, :]
                                .broadcast_to([128, 4, 511]), ALU.mult))
                            wV(sv)
                            vI(V_.tensor_tensor(a4[:, :, 1:512],
                                                a4[:, :, 1:512],
                                                t4[:, :, 1:512], ALU.add))
                            vI(V_.tensor_tensor(
                                t4[:, :, 0:511], src[:, :, 1:512],
                                w2b_[:, h0:h0 + 4, :]
                                .broadcast_to([128, 4, 511]), ALU.mult))
                            wV(sv)
                            vI(V_.tensor_tensor(a4[:, :, 0:511],
                                                a4[:, :, 0:511],
                                                t4[:, :, 0:511], ALU.add))
                            b0_, b1_ = 2 * tq, 2 * tq + 1
                            hs = halT[:, :].rearrange("p (hh s) -> p hh s",
                                                      s=16)
                            for (col, tap, bb, dirr, sub_col) in (
                                    (0, w0b, b0_, 0, None),
                                    (256, w0b, b1_, 0, 255),
                                    (255, w2b_, b0_, 1, 256),
                                    (511, w2b_, b1_, 1, None)):
                                wV(sv)
                                vI(V_.tensor_tensor(
                                    t4[:, :, col:col + 1],
                                    hs[:, h0:h0 + 4, bb * 2 + dirr:
                                       bb * 2 + dirr + 1],
                                    tap[:, h0:h0 + 4, :]
                                    .broadcast_to([128, 4, 1]), ALU.mult))
                                wV(sv)
                                vI(V_.tensor_tensor(a4[:, :, col:col + 1],
                                                    a4[:, :, col:col + 1],
                                                    t4[:, :, col:col + 1],
                                                    ALU.add))
                                if sub_col is not None:
                                    vI(V_.tensor_tensor(
                                        t4[:, :, col:col + 1],
                                        src[:, :, sub_col:sub_col + 1],
                                        tap[:, h0:h0 + 4, :]
                                        .broadcast_to([128, 4, 1]), ALU.mult))
                                    wV(sv)
                                    vI(V_.tensor_scalar(
                                        out=t4[:, :, col:col + 1],
                                        in0=t4[:, :, col:col + 1],
                                        scalar1=-1.0, scalar2=None,
                                        op0=ALU.mult))
                                    wV(sv)
                                    vI(V_.tensor_tensor(a4[:, :, col:col + 1],
                                                        a4[:, :, col:col + 1],
                                                        t4[:, :, col:col + 1],
                                                        ALU.add))
                            for hh in range(4):
                                wC(sv)
                                cI(C_.activation(
                                    hv[:, h0 + hh, :],
                                    a4[:, hh, :], ACT.Silu,
                                    bias=cvbs[:, h0 + hh:h0 + hh + 1],
                                    scale=1.0))
                        if debug and tq == 0:
                            S_.wait_ge(ss.h, ss.n)
                            S_.wait_ge(sv.h, sv.n)
                            dO(dbg5.ap()[0:128, 16384:24576], hq[:, 0:8192])
                            bar()
                        bar()
                        wC(ss)
                        vI(V_.tensor_copy(accm[:, 0:512], hv[:, 0, :]))
                        vI(V_.tensor_copy(accn[:, 0:512], hv[:, 0, :]))
                        wV(sv)
                        for hcx in range(1, 32):
                            vI(V_.tensor_tensor(accm[:, 0:512], accm[:, 0:512],
                                                hv[:, hcx, :], ALU.max))
                            vI(V_.tensor_tensor(accn[:, 0:512], accn[:, 0:512],
                                                hv[:, hcx, :], ALU.min))
                        wV(sv)
                        vI(V_.tensor_scalar(out=accn[:, 0:512],
                                            in0=accn[:, 0:512], scalar1=-1.0,
                                            scalar2=None, op0=ALU.mult))
                        wV(sv)
                        vI(V_.tensor_tensor(accm[:, 0:512], accm[:, 0:512],
                                            accn[:, 0:512], ALU.max))
                        bar()
                        token_stats(accm, 512, 4)
                        bcast2(s1b[:, 0:512], accn[:, 0:512], 512)
                        for hcx in range(32):
                            wV(ss, sv)
                            vI(V_.tensor_tensor(accm[:, 0:512], hv[:, hcx, :],
                                                s1b[:, 0:512], ALU.mult))
                            wV(sv)
                            vI(V_.tensor_scalar(out=hv[:, hcx, :],
                                                in0=accm[:, 0:512],
                                                scalar1=RNE, scalar2=RNE,
                                                op0=ALU.add, op1=ALU.subtract))
                        if debug and tq == 0:
                            S_.wait_ge(sv.h, sv.n)
                            dO(dbg5.ap()[0:128, 24576:32768], hq[:, 0:8192])
                            bar()
                        bar()
                        dS(wqb[:, 0:4096].rearrange(
                               "p (a q) -> p a q", a=32),
                           w2q_d.ap()[:, :, 0:128]
                           .rearrange("a p q -> p a q"))
                        pfw = {0: sd.n}
                        w2vb = sv.n
                        for mcx in range(DC):
                            PS = ps_a if mcx % 2 == 0 else ps_b
                            if mcx + 1 < DC:
                                hb1 = ((mcx + 1) % 2) * 4096
                                S_.wait_ge(st.h, st.n)
                                dS(wqb[:, hb1:hb1 + 4096].rearrange(
                                       "p (a q) -> p a q", a=32),
                                   w2q_d.ap()[:, :, (mcx + 1) * 128:
                                              (mcx + 2) * 128]
                                   .rearrange("a p q -> p a q"))
                                pfw[mcx + 1] = sd.n
                            T_.wait_ge(sd.h, pfw[mcx])
                            if mcx >= 2:
                                T_.wait_ge(sv.h, w2vb + 2 * (mcx - 1))
                            for hcx in range(32):
                                MM(PS[:, 0:512],
                                          wqb[:, (mcx % 2) * 4096 + hcx * 128:
                                              (mcx % 2) * 4096 +
                                              (hcx + 1) * 128],
                                          hv[:, hcx, :],
                                          start=(hcx == 0), stop=(hcx == 31))
                            tmark()
                            wC(st, sv)
                            cI(C_.activation(accm[:, 0:512], PS[:, 0:512],
                                             ACT.Identity, bias=0.0,
                                             scale=sw2s[:, 0:1]))
                            wV(ss)
                            vI(V_.tensor_tensor(accm[:, 0:512], accm[:, 0:512],
                                                accn[:, 0:512], ALU.mult))
                            wV(sv)
                            vI(V_.tensor_tensor(
                                strm[:, mcx * 2048 + tc0:
                                     mcx * 2048 + tc0 + 512],
                                strm[:, mcx * 2048 + tc0:
                                     mcx * 2048 + tc0 + 512],
                                accm[:, 0:512], ALU.add))
                        bar()
                _stk8c.close()
                _stk8.close()
                _stkF = ExitStack()
                sqt = _stkF.enter_context(nc.sbuf_tensor("sqtF", [128, 1024], F32))
                sA = _stkF.enter_context(nc.sbuf_tensor("sAF", [128, 2048], F32))
                sB = _stkF.enter_context(nc.sbuf_tensor("sBF", [128, 2048], F32))
                bar()
                ln_fm(strm, 1e-5, sqt, sA, sB)
                for dcx in range(DC):
                    dO(out_own[dcx * 128:(dcx + 1) * 128, :],
                       strm[:, dcx * 2048:(dcx + 1) * 2048])
                bar()
                _stkF.close()
            _stk6.close()
    _top.close()
    for cm in reversed(persist):
        cm.__exit__(None, None, None)
    return nc


def _stage(core, x, W):
    sl = slice(core * SO, (core + 1) * SO)
    d = {}
    d["xt"] = np.ascontiguousarray(
        x[:, sl, :].transpose(2, 0, 1).reshape(D, T))
    relp = np.concatenate(
        [W["rel"], np.tile(W["rel"][2046:2047], (2048, 1))], 0)
    hh = np.arange(H)
    ii = np.arange(S)[None, None, :]
    jj = np.arange(S)[None, :, None]
    iq = np.arange(core * SO, (core + 1) * SO)
    idx = ((iq[None, None, :] % 16) * 128 + jj // 16
           - (hh[:, None, None] * 128 + iq[None, None, :] // 16) + 2047)
    d["bt"] = np.ascontiguousarray(
        relp[idx, (jj % 16)].astype(np.float32))
    d["wq0"], d["wk0"], d["wv0"] = W["Wq0"], W["Wk0"], W["Wv0"]
    d["g0"] = np.ascontiguousarray(W["g0"].reshape(DC, 128).T)
    d["b0"] = np.ascontiguousarray(W["b0"].reshape(DC, 128).T)
    d["wc"] = np.ascontiguousarray(W["Wc"].reshape(D, ES))
    d["bcb"] = np.ascontiguousarray(W["bc"].reshape(ES, 1))
    W1 = W["W1"].reshape(ES, D, FF)
    d["w1s"] = np.ascontiguousarray(
        W1.reshape(4, 32, DC, 128, FF).transpose(0, 2, 3, 1, 4))
    b1 = W["b1"].reshape(ES, FF)
    b1p = np.zeros((128, ES), np.float32)
    b1p[0:32, :] = b1.T
    b1p[64:96, :] = b1.T
    d["b1p"] = b1p
    W2 = W["W2"].reshape(ES, D, FF)
    w2s = np.zeros((4, 128, 16, DC, 128), np.float32)
    for j in range(4):
        for pair in range(16):
            for sub in range(2):
                es = j * 32 + pair * 2 + sub
                wt = W2[es].reshape(DC, 128, FF)
                w2s[j, sub * 64:sub * 64 + 32, pair] = wt.transpose(2, 0, 1)
    d["w2s"] = w2s
    d["b2p"] = np.ascontiguousarray(W["b2"].reshape(ES, D))
    d["wq1"], d["wk1"], d["wv1"], d["wp1"] = (W["Wq1"], W["Wk1"], W["Wv1"],
                                              W["Wp1"])
    d["bq1"] = np.ascontiguousarray(W["bq1"].reshape(DC, 128).T)
    d["bp1"] = np.ascontiguousarray(W["bp1"].reshape(DC, 128).T)
    d["bk1r"] = W["bk1"].reshape(1, D)
    d["bv1r"] = W["bv1"].reshape(1, D)
    d["w1t"] = np.ascontiguousarray(W["w1"].T)
    d["w2t"] = np.ascontiguousarray(W["w2"].T)
    d["cvw"] = np.ascontiguousarray(
        W["cw"].reshape(32, 128, 3).transpose(1, 0, 2))
    d["cvb"] = np.ascontiguousarray(W["cb"].reshape(32, 128).T)
    hs = np.zeros((128, 16), np.float32)
    for b in range(B):
        if core > 0:
            hs[(core - 1) * 16 + b * 2 + 1, b * 2 + 0] = 1.0
        if core < NCORE - 1:
            hs[(core + 1) * 16 + b * 2 + 0, b * 2 + 1] = 1.0
    d["hsel"] = hs
    return d


_NC_CACHE = {}


def kernel(**inputs):
    W = dict(
        Wq0=inputs["L0_attn_Wq"], Wk0=inputs["L0_attn_Wk"],
        Wv0=inputs["L0_attn_Wv"], rel=inputs["L0_attn_rel_table"],
        g0=inputs["L0_attn_gamma"], b0=inputs["L0_attn_beta"],
        Wc=inputs["L0_ffn_Wc"], bc=inputs["L0_ffn_bc"],
        W1=np.ascontiguousarray(
            inputs["L0_ffn_W1"].transpose(0, 1, 2, 3)
            .reshape(32, 4, D, FF).transpose(0, 1, 2, 3)
            .reshape(32 * 4, D, FF)
            .reshape(ES, D, FF)) if False else np.ascontiguousarray(
            inputs["L0_ffn_W1"].reshape(32, 4, D, FF)
            .transpose(0, 1, 2, 3).reshape(ES, D, FF)),
        b1=np.ascontiguousarray(inputs["L0_ffn_b1"].reshape(ES, FF)),
        W2=np.ascontiguousarray(inputs["L0_ffn_W2"].reshape(ES, D, FF)),
        b2=np.ascontiguousarray(inputs["L0_ffn_b2"].reshape(ES, D)),
        Wq1=inputs["L1_attn_Wq"], bq1=inputs["L1_attn_bq"],
        Wk1=inputs["L1_attn_Wk"], bk1=inputs["L1_attn_bk"],
        Wv1=inputs["L1_attn_Wv"], bv1=inputs["L1_attn_bv"],
        Wp1=inputs["L1_attn_Wp"], bp1=inputs["L1_attn_bp"],
        w1=inputs["L1_ffn_w1"], cw=inputs["L1_ffn_conv_w"],
        cb=inputs["L1_ffn_conv_b"], w2=inputs["L1_ffn_w2"],
    )
    W = {k: np.ascontiguousarray(np.asarray(v, dtype=np.float32))
         for k, v in W.items()}
    x = np.ascontiguousarray(np.asarray(inputs["L0_attn_x"], np.float32))
    in_maps = [_stage(c, x, W) for c in range(NCORE)]
    if "nc" not in _NC_CACHE:
        _NC_CACHE["nc"] = build()
    res = run_bass_kernel_spmd(_NC_CACHE["nc"], in_maps,
                               core_ids=list(range(NCORE)))
    out = np.empty((B, S, D), np.float32)
    for c in range(NCORE):
        o = res.results[c]["out_own"]
        out[:, c * SO:(c + 1) * SO, :] = (
            o.reshape(D, B, SO).transpose(1, 2, 0))
    return out
